# version 15
# baseline (speedup 1.0000x reference)
import contextlib
import numpy as np

import concourse.bass as bass
import concourse.mybir as mybir
from concourse.bass_utils import run_bass_kernel_spmd

F32 = mybir.dt.float32
BF16 = mybir.dt.bfloat16
AF = mybir.ActivationFunctionType
ALU = mybir.AluOpType

C = 1024
CH = 8
TCORE = 2048
TILE = 256
NT = TCORE // TILE
H = 16
EPS_LN = 1e-5
EPS_ATT = 1e-6
PAIRS = [[0, 1], [2, 3], [4, 5], [6, 7]]

SBUF_BASE = 20608
SBUF_END = 229344

ALIAS = {
    "xr": ["u2a", "pfm"], "u2a": ["xr", "pfm"], "pfm": ["xr", "u2a"],
    "xk": ["u2b", "t2a_lo"], "u2b": ["xk", "t2a_lo"], "t2a_lo": ["xk", "u2b"],
    "rows": ["t2a_hi"], "mbsb": ["t2a_hi"], "t2a_hi": ["rows", "mbsb"],
    "u": ["ebuf", "rbuf", "qtok", "ptok", "ebuf2", "iota_j", "iota_p",
          "kvtmp"],
    "ebuf": ["u", "kvtmp"], "rbuf": ["u", "kvtmp"], "qtok": ["u", "kvtmp"],
    "ptok": ["u", "kvtmp", "ebuf2"],
    "iota_j": ["u", "kvtmp"], "iota_p": ["u", "kvtmp"],
    "kvtmp": ["u", "ebuf", "rbuf", "qtok", "ptok", "ebuf2",
              "iota_j", "iota_p"],
    "ebuf2": ["u", "ptok", "kvtmp"],
    "rg": ["Rwp"], "kact": ["Rwp"],
    "kact1": ["Rwp"], "kact2": ["Rwp"], "kact3": ["Rwp"], "kact4": ["Rwp"],
    "dbuf": ["t2", "Rwp"], "t2": ["dbuf", "Rwp"],
    "Rwp": ["rg", "kact", "kact1", "kact2", "kact3", "kact4", "dbuf", "t2"],
    "kext0": ["Rwvf"], "kext1": ["Rwvf"], "vext0": ["Rwvf"], "vext1": ["Rwvf"],
    "kv_f": ["Rwvf"],
    "Rwvf": ["kext0", "kext1", "vext0", "vext1", "kv_f"],
    "qfm": ["Rwr", "Rwkf"], "Rwr": ["qfm"], "Rwkf": ["qfm"],
    "Rwr2": ["Rwq"], "Rwq": ["Rwr2"],
    "Rwkf2": ["Rwk"], "Rwk": ["Rwkf2"],
    "Rwvf2": ["Rwv"], "Rwv": ["Rwvf2"],
    "b6": ["b6d0", "b6d1"], "b6d0": ["b6"], "b6d1": ["b6"],
}


class FW:

    def __init__(self, nc):
        self.nc = nc
        self.ops = []
        self.lastw = {}
        self.readers = {}
        self.n_cc = 0

    def op(self, eng, fn, reads=(), writes=(), kind="c", fence=False):
        wx = list(writes)
        for w in writes:
            wx += ALIAS.get(w, [])
        deps = set()
        for r in reads:
            if r in self.lastw:
                deps.add(self.lastw[r])
        for w in wx:
            if w in self.lastw:
                deps.add(self.lastw[w])
            deps.update(self.readers.get(w, ()))
        i = len(self.ops)
        ccid = None
        if kind == "cc":
            ccid = self.n_cc
            self.n_cc += 1
        self.ops.append(dict(eng=eng, kind=kind, fn=fn, deps=sorted(deps), ccid=ccid,
                             fence=fence))
        for r in reads:
            self.readers.setdefault(r, []).append(i)
        for w in writes:
            self.lastw[w] = i
            self.readers[w] = []
        return i

    def dma(self, eng, fn, reads=(), writes=()):
        return self.op(eng, fn, reads, writes, kind="d")

    def cc(self, fn, reads=(), writes=()):
        return self.op("pool", fn, reads, writes, kind="cc")

    def emit(self, block, stack):
        nc = self.nc
        sem_c = {e: stack.enter_context(nc.semaphore(f"sc_{e}"))
                 for e in ("pe", "act", "dve", "pool", "sp")}
        sem_d = {e: stack.enter_context(nc.semaphore(f"sd_{e}"))
                 for e in ("sp", "pool", "act")}
        cc_sems = [stack.enter_context(nc.semaphore(f"cc{i}")) for i in range(self.n_cc)]

        seq = {}
        counters = {}
        for i, o in enumerate(self.ops):
            if o["kind"] == "cc":
                continue
            key = (o["eng"], o["kind"])
            n = counters.get(key, 0)
            counters[key] = n + 1
            seq[i] = n

        def token(i):
            o = self.ops[i]
            if o["kind"] == "cc":
                return (cc_sems[o["ccid"]], 1)
            if o["kind"] == "c":
                return (sem_c[o["eng"]], seq[i] + 1)
            return (sem_d[o["eng"]], 16 * (seq[i] + 1))

        def emit_for(name):
            def run(eng):
                waited = {}
                for i, o in enumerate(self.ops):
                    if o["eng"] != name:
                        continue
                    want = {}
                    for d in o["deps"]:
                        p = self.ops[d]
                        if p["eng"] == name == "pe" and p["kind"] == "c":
                            continue
                        sem, val = token(d)
                        if want.get(sem, 0) < val:
                            want[sem] = val
                    for sem, val in sorted(want.items(), key=lambda kv: kv[0].num):
                        if waited.get(sem, 0) < val:
                            eng.wait_ge(sem, val)
                            waited[sem] = val
                    inst = o["fn"](eng)
                    if o["kind"] == "c":
                        inst.then_inc(sem_c[name], 1)
                    elif o["kind"] == "d":
                        inst.then_inc(sem_d[name], 16)
                    else:
                        inst.then_inc(cc_sems[o["ccid"]])
                if name == "sp":
                    for e in ("sp", "pool", "act"):
                        t = counters.get((e, "d"), 0)
                        if t:
                            eng.wait_ge(sem_d[e], 16 * t)
                    for s in cc_sems:
                        eng.wait_ge(s, 1)
            return run

        block.tensor(emit_for("pe"))
        block.scalar(emit_for("act"))
        block.vector(emit_for("dve"))
        block.gpsimd(emit_for("pool"))
        block.sync(emit_for("sp"))


def build_program(upto="full", dbg=False, no_bnd=False, a2lim=0):
    nc = bass.Bass("TRN2", target_bir_lowering=False, debug=False)

    xT = nc.dram_tensor("xT", [C, TCORE + 1], F32, kind="ExternalInput").ap()
    wdram = {}
    for name in ("wqT", "wkT", "wvT", "wpT", "wrT", "wkfT", "wvfT"):
        wdram[name] = nc.dram_tensor(name, [C, C], F32, kind="ExternalInput").ap()
    mur_in = nc.dram_tensor("mur8", [CH, 128], F32, kind="ExternalInput").ap()
    muk_in = nc.dram_tensor("muk8", [CH, 128], F32, kind="ExternalInput").ap()
    bias_in = nc.dram_tensor("bias8", [CH, 128], F32, kind="ExternalInput").ap()
    mask_in = nc.dram_tensor("mask8", [128, CH], F32, kind="ExternalInput").ap()
    om_in = nc.dram_tensor("om8", [128, CH], F32, kind="ExternalInput").ap()
    yT = nc.dram_tensor("yT", [C, TCORE], F32, kind="ExternalOutput").ap()
    cc_kv_in = nc.dram_tensor("cc_kv_in", [65, H * 65], F32)
    cc_kv_out = nc.dram_tensor("cc_kv_out", [65, H * 65], F32)
    cc_kv_in2 = nc.dram_tensor("cc_kv_in2", [65, H * 65], F32)
    cc_kv_out2 = nc.dram_tensor("cc_kv_out2", [65, H * 65], F32)
    cc_b_in = nc.dram_tensor("cc_b_in", [128, CH], F32)
    cc_b_out = nc.dram_tensor("cc_b_out", [128, CH], F32)
    srow_d = nc.dram_tensor("srow_d", [1, TILE], F32)
    if dbg:
        dbg_o = nc.dram_tensor("dbg", [128, 96], F32, kind="ExternalOutput").ap()

    S = {}

    def at(name, shape, dtype, off):
        assert SBUF_BASE <= off and off + shape[1] * mybir.dt.size(dtype) <= SBUF_END, name
        S[name] = nc.alloc_sbuf_tensor_at(name, shape, dtype, offset=off).ap()
        return off

    cur = SBUF_BASE

    def alloc(name, shape, dtype):
        nonlocal cur
        cur = (cur + 31) // 32 * 32
        off = at(name, shape, dtype, cur)
        cur += shape[1] * mybir.dt.size(dtype)
        return off

    alloc("X", [128, CH * TCORE], F32)
    off_wr = alloc("wr", [128, CH * C], BF16)
    at("qfm", [128, CH * TCORE], BF16, off_wr)
    alloc("wkf", [128, CH * C], BF16)
    off_wvf = alloc("wvf", [128, CH * C], BF16)
    at("kext0", [128, 1040], BF16, off_wvf)
    at("kext1", [128, 1040], BF16, off_wvf + 2080)
    at("vext0", [128, 1040], BF16, off_wvf + 4160)
    at("vext1", [128, 1040], BF16, off_wvf + 6240)
    at("kv_f", [128, H * 65], F32, off_wvf + 8320)
    off_wq = alloc("wq", [128, CH * C], BF16)

    at("wr2", [128, CH * C], BF16, off_wq)
    off_wk = alloc("wk", [128, CH * C], BF16)
    at("wkf2", [128, CH * C], BF16, off_wk)
    off_wv = alloc("wv", [128, CH * C], BF16)
    at("wvf2", [128, CH * C], BF16, off_wv)
    off_wp = alloc("wp", [128, CH * C], BF16)
    at("rg", [128, CH * TILE], BF16, off_wp)
    at("kact", [128, CH * TILE], BF16, off_wp + 4096)
    off_big = off_wp + 8192
    at("dbuf", [128, CH * TILE], F32, off_big)
    at("t2", [128, CH * TILE], F32, off_big)

    alloc("ident_bf", [128, 128], BF16)
    alloc("ones_inv", [128, 1], BF16)
    alloc("ones_row", [1, 128], F32)
    alloc("mask_b", [128, CH], F32)
    alloc("om_b", [128, CH], F32)
    alloc("mu_r", [128, CH], F32)
    alloc("mu_k", [128, CH], F32)
    alloc("bias", [128, CH], F32)
    alloc("uhalo0", [128, CH], F32)
    alloc("uhalo1", [128, CH], F32)
    alloc("halo1", [128, CH], F32)
    alloc("halo2w", [128, CH], F32)
    alloc("halo2", [128, CH], F32)
    alloc("bndx", [128, CH], F32)
    alloc("hsrc_bf", [128, CH], BF16)
    alloc("hsrcX", [128, CH], F32)
    alloc("s_tok", [128, 2 * NT], F32)
    off_kvbf = alloc("kv_bf", [128, H * 65], BF16)
    alloc("zden", [128, H], F32)
    alloc("zrec", [128, H], F32)
    alloc("zden1", [128, H], F32)
    alloc("zrec1", [128, H], F32)
    alloc("hrow", [1, 8], F32)
    alloc("hmb", [128, 2], F32)
    alloc("uhsp", [128, CH], F32)
    alloc("ceps", [128, 1], F32)
    alloc("cepsa", [128, 1], F32)


    alloc("x_bf", [128, 4 * TILE], BF16)
    off_rows = alloc("rows", [1, 2 * TILE], F32)
    at("mbsb", [128, 2 * TILE], F32, off_rows + 2048)
    cur += 2048
    at("t2a_hi", [128, 4 * TILE], F32, off_rows)
    off_u = alloc("u", [128, CH * (TILE + 1)], F32)
    at("ebuf", [128, 512], F32, off_u)
    at("rbuf", [128, 512], F32, off_u + 2048)
    at("qtok", [128, 1024], BF16, off_u + 4096)
    at("ptok", [128, 1024], BF16, off_u + 6144)
    at("ebuf2", [128, 512], F32, off_u + 6144)
    at("iota_j", [128, 128], F32, off_u)
    at("iota_p", [128, 1], F32, off_u + 512)
    at("kvtmp", [65, H * 65], F32, off_u)
    at("m2row", [1, TILE], F32, off_u)
    at("sxrow", [1, TILE], F32, off_u + 1024)
    off_xr = alloc("xr", [128, CH * TILE], BF16)
    at("u2a", [128, CH * TILE], BF16, off_xr)
    at("pfm", [128, CH * TILE], BF16, off_xr)
    off_xk = alloc("xk", [128, CH * TILE], BF16)
    at("u2b", [128, CH * TILE], BF16, off_xk)
    at("t2a_lo", [128, 4 * TILE], F32, off_xk)
    off_trelu = alloc("trelu0", [128, TILE], F32)
    alloc("trelu1", [128, TILE], F32)
    at("treluW", [128, 2 * TILE], F32, off_trelu)
    assert cur <= SBUF_END, f"SBUF overflow: {cur} > {SBUF_END}"

    stack = contextlib.ExitStack()
    PS = {}
    for i in range(7):
        PS[f"b{i}"] = stack.enter_context(nc.psum_tensor(f"psb{i}", [128, 512], F32))
    PS["b7"] = stack.enter_context(nc.psum_tensor("psb7", [128, 1024], BF16))

    fw = FW(nc)
    Xv = S["X"][:].rearrange("p (c t) -> p c t", c=CH)

    def tap(c0, view, reads):
        if dbg:
            fw.dma("sp", (lambda e: e.dma_start(out=dbg_o[:, c0:c0 + view.shape[-1] if view.ndim == 2 else c0 + view.shape[1]], in_=view)
                   if view.ndim == 2 else None), reads=reads, writes=[f"dbgt{c0}"])

    def tap2(c0, mk, reads, q="sp"):
        if dbg:
            fw.dma(q, mk, reads=reads, writes=[f"dbgt{c0}"])

    for j in range(CH):
        fw.dma("sp", (lambda e, j=j: e.dma_start(
            out=S["X"][:, j * TCORE:(j + 1) * TCORE],
            in_=xT[j * 128:(j + 1) * 128, 1:TCORE + 1])),
            writes=[f"Xload{j}"])
    fw.op("sp", lambda e: e.nop(),
          reads=[f"Xload{j}" for j in range(CH)],
          writes=[f"X{k}" for k in range(NT)])

    fw.dma("sp", lambda e: e.dma_start(
        out=S["halo1"][:], in_=xT[:, 0:1].rearrange("(j p) o -> p (j o)", p=128),
        allow_slow_non_contiguous=True), writes=["halo1"])
    fw.dma("sp", lambda e: e.dma_start(out=S["mask_b"][:], in_=mask_in),
           writes=["mask_b"])
    fw.dma("sp", lambda e: e.dma_start(out=S["om_b"][:], in_=om_in),
           writes=["om_b"])
    fw.dma("sp", lambda e: e.dma_start(out=S["mu_r"][:], in_=mur_in.rearrange("j p -> p j"),
                                       allow_slow_non_contiguous=True), writes=["mu_r"])
    fw.dma("sp", lambda e: e.dma_start(out=S["mu_k"][:], in_=muk_in.rearrange("j p -> p j"),
                                       allow_slow_non_contiguous=True), writes=["mu_k"])
    fw.dma("sp", lambda e: e.dma_start(out=S["bias"][:], in_=bias_in.rearrange("j p -> p j"),
                                       allow_slow_non_contiguous=True), writes=["bias"])

    fw.op("pool", lambda e: e.iota(S["iota_j"][:], pattern=[[1, 128]], base=0,
                                   channel_multiplier=0, allow_small_or_imprecise_dtypes=True),
          writes=["iota_j"])
    fw.op("pool", lambda e: e.iota(S["iota_p"][:], pattern=[[0, 1]], base=0,
                                   channel_multiplier=1, allow_small_or_imprecise_dtypes=True),
          writes=["iota_p"])
    fw.op("dve", lambda e: e.tensor_scalar(S["ident_bf"][:], S["iota_j"][:], S["iota_p"][:],
                                           None, ALU.is_equal),
          reads=["iota_j", "iota_p"], writes=["ident_bf"])
    fw.op("dve", lambda e: e.memset(S["ones_inv"][:], 1.0 / C), writes=["ones_inv"])
    fw.op("dve", lambda e: e.memset(S["ones_row"][0:1, :], 1.0), writes=["ones_row"])
    fw.op("dve", lambda e: e.memset(S["ceps"][:], EPS_LN), writes=["ceps"])
    fw.op("dve", lambda e: e.memset(S["cepsa"][:], EPS_ATT), writes=["cepsa"])
    fw.op("dve", lambda e: e.tensor_tensor(S["halo1"][:], S["halo1"][:],
                                           S["mask_b"][:], ALU.mult),
          reads=["mask_b", "halo1"], writes=["halo1"], fence=True)

    def wload(dst, src, res, extra_reads=(), j0=0, j1=CH, fin=True):
        for j in range(j0, j1):
            fw.dma("pool", (lambda e, j=j: e.dma_start(
                out=S[dst][:, j * C:(j + 1) * C],
                in_=wdram[src][j * 128:(j + 1) * 128, :])),
                reads=list(extra_reads), writes=[f"{res}_{j}"])
        if fin:
            fw.op("pool", lambda e: e.nop(),
                  reads=[f"{res}_{j}" for j in range(CH)], writes=[res])

    wload("wr", "wrT", "Rwr")
    wload("wkf", "wkfT", "Rwkf")
    wload("wvf", "wvfT", "Rwvf")
    fw.op("dve", lambda e: e.tensor_scalar(S["wvf"][:], S["wvf"][:], 0.5, None, ALU.mult),
          reads=["Rwvf"], writes=["Rwvf"])
    wload("wq", "wqT", "Rwq")
    wload("wk", "wkT", "Rwk")
    wload("wv", "wvT", "Rwv")

    def ln_quarters(k, fake_src=None):
        c0 = k * TILE

        def castq(q):
            h = q % 2
            dst = S["x_bf"][:, h * 512:h * 512 + 512].rearrange(
                "p (c t) -> p c t", c=2)
            if fake_src is None:
                fw.dma("pool", (lambda e, q=q, dst=dst: e.dma_start(
                    out=dst, in_=Xv[:, 2 * q:2 * q + 2, c0:c0 + TILE])),
                    reads=[f"X{k}"], writes=[f"x_bf{h}"])
            else:
                fw.op("dve", (lambda e, q=q, dst=dst: e.tensor_copy(
                    dst, S[fake_src][:, 2 * q:2 * q + 2].unsqueeze(2)
                    .broadcast_to([128, 2, TILE]))),
                    reads=[fake_src], writes=[f"x_bf{h}"])

        def summsq(q):
            h = q % 2

            def f(e, h=h, q=q):
                last = None
                for jj in range(2):
                    last = e.matmul(PS["b0"][0:1, 0:TILE], S["ones_inv"][:, 0:1],
                                    S["x_bf"][:, h * 512 + jj * TILE:
                                               h * 512 + (jj + 1) * TILE],
                                    start=(q == 0 and jj == 0),
                                    stop=(q == 3 and jj == 1))
                return last
            fw.op("pe", f, reads=[f"x_bf{h}"], writes=["b0"])

        def sqq(q):
            h = q % 2
            v = S["x_bf"][:, h * 512:h * 512 + 512]
            fw.op("act", (lambda e, v=v: e.activation(v, v, AF.Square)),
                  reads=[f"x_bf{h}"], writes=[f"x_bf{h}"])

        def sqmmsq(q):
            h = q % 2

            def f(e, h=h, q=q):
                last = None
                for jj in range(2):
                    last = e.matmul(PS["b0"][0:1, TILE:2 * TILE],
                                    S["ones_inv"][:, 0:1],
                                    S["x_bf"][:, h * 512 + jj * TILE:
                                               h * 512 + (jj + 1) * TILE],
                                    start=False, stop=(q == 3 and jj == 1))
                return last
            fw.op("pe", f, reads=[f"x_bf{h}"], writes=["b0"])

        def rowblob():
            sqmmsq(2)
            sqmmsq(3)
            fw.op("dve", lambda e: e.tensor_copy(S["rows"][0:1, :],
                                                 PS["b0"][0:1, 0:2 * TILE]),
                  reads=["b0"], writes=["rows"])
            fw.op("dve", (lambda e: e.tensor_tensor(
                S["x_bf"][0:1, 0:TILE], S["rows"][0:1, 0:TILE],
                S["rows"][0:1, 0:TILE], ALU.mult)),
                reads=["rows"], writes=["x_bf0"])
            fw.op("dve", (lambda e: e.tensor_tensor(
                S["rows"][0:1, TILE:2 * TILE], S["rows"][0:1, TILE:2 * TILE],
                S["x_bf"][0:1, 0:TILE], ALU.subtract)),
                reads=["x_bf0"], writes=["rows"])

        def rsqrt1():
            fw.op("act", lambda e: e.activation(S["rows"][0:1, TILE:2 * TILE],
                                                S["rows"][0:1, TILE:2 * TILE],
                                                AF.Sqrt, bias=S["ceps"][0:1, :]),
                  reads=["ceps"], writes=["rows"])

        def rsqrt2():
            fw.op("dve", lambda e: e.reciprocal(S["rows"][0:1, TILE:2 * TILE],
                                                S["rows"][0:1, TILE:2 * TILE]),
                  reads=[], writes=["rows"])

        return {
            "P0": lambda: (castq(0), castq(1)),
            "P1": lambda: (summsq(0), sqq(0)),
            "P2": lambda: (summsq(1), sqq(1), sqmmsq(0), castq(2)),
            "P3": lambda: (summsq(2), sqq(2), sqmmsq(1), castq(3)),
            "P4": lambda: (summsq(3), sqq(3)),
            "C": rowblob,
            "D1": rsqrt1, "D2": rsqrt2,
        }

    def ln_bcast(both):
        def f(e):
            r = e.matmul(PS["b0"][:, 0:TILE], S["ones_row"][0:1, :],
                         S["rows"][0:1, 0:TILE], start=True, stop=True)
            if both:
                r = e.matmul(PS["b0"][:, TILE:2 * TILE], S["ones_row"][0:1, :],
                             S["rows"][0:1, TILE:2 * TILE],
                             start=False, stop=True)
            return r
        fw.op("pe", f, reads=["rows", "ones_row"], writes=["b0"])

    uview = S["u"][:].rearrange("p (c t) -> p c t", c=CH)

    def halo_special(hsrc):
        lq = ln_quarters(0, fake_src=hsrc)
        for nm in ("P0", "P1", "P2", "P3", "P4", "C", "D1", "D2"):
            lq[nm]()
        ln_bcast(True)
        fw.op("dve", lambda e: e.tensor_copy(S["mbsb"][:, :], PS["b0"][:, :]),
              reads=["b0"], writes=["mbsb"])
        fw.op("dve", (lambda e: e.tensor_tensor(
            S["uhsp"][:], S[hsrc][:],
            S["mbsb"][:, 0:1].broadcast_to([128, CH]), ALU.subtract)),
            reads=[hsrc, "mbsb"], writes=["uhsp"])
        fw.op("dve", (lambda e: e.tensor_tensor(
            S["uhsp"][:], S["uhsp"][:],
            S["mbsb"][:, TILE:TILE + 1].broadcast_to([128, CH]), ALU.mult)),
            reads=["mbsb"], writes=["uhsp"], fence=True)

    def ffn_pieces(k, chain, hsrc):
        c0 = k * TILE
        dv = S["dbuf"][:].rearrange("p (c t) -> p c t", c=CH)
        xrv = S["xr"][:].rearrange("p (c t) -> p c t", c=CH)
        xkv = S["xk"][:].rearrange("p (c t) -> p c t", c=CH)
        lq = ln_quarters(k)

        def P0():
            if not chain:
                halo_special(hsrc)
            lq["P0"]()

        def T1():
            fw.op("dve", lambda e: e.tensor_copy(S["mbsb"][:, :], PS["b0"][:, :]),
                  reads=["b0"], writes=["mbsb"])

        def T2():
            fw.op("dve", (lambda e: e.tensor_tensor(
                uview[:, :, 1:TILE + 1], Xv[:, :, c0:c0 + TILE],
                S["mbsb"][:, 0:TILE].unsqueeze(1).broadcast_to([128, CH, TILE]),
                ALU.subtract)),
                reads=[f"X{k}", "mbsb"], writes=["u"])
            fw.op("dve", (lambda e: e.tensor_tensor(
                uview[:, :, 1:TILE + 1], uview[:, :, 1:TILE + 1],
                S["mbsb"][:, TILE:2 * TILE].unsqueeze(1).broadcast_to([128, CH, TILE]),
                ALU.mult)),
                reads=["mbsb"], writes=["u"])

        def T3():
            uh_r = f"uhalo{(k - 1) % 2}" if chain else "uhsp"
            fw.op("dve", (lambda e, uh_r=uh_r: e.tensor_copy(
                uview[:, :, 0:1], S[uh_r][:].unsqueeze(2))),
                reads=[uh_r], writes=["u"], fence=True)
            uh_w = f"uhalo{k % 2}"
            fw.op("dve", (lambda e, uh_w=uh_w: e.tensor_copy(
                S[uh_w][:].unsqueeze(2), uview[:, :, TILE:TILE + 1])),
                reads=["u"], writes=[uh_w], fence=True)
            fw.op("dve", (lambda e: e.tensor_tensor(
                dv, uview[:, :, 1:TILE + 1], uview[:, :, 0:TILE], ALU.subtract)),
                reads=["u"], writes=["dbuf"])

        def T4():
            def mr(e):
                last = None
                for j in range(CH):
                    last = e.scalar_tensor_tensor(xrv[:, j, :], dv[:, j, :],
                                                  S["mu_r"][:, j:j + 1],
                                                  uview[:, j, 0:TILE],
                                                  ALU.mult, ALU.add)
                return last
            fw.op("dve", mr, reads=["dbuf", "u", "mu_r"], writes=["xr"])

            def mk(e):
                last = None
                for j in range(CH):
                    last = e.scalar_tensor_tensor(xkv[:, j, :], dv[:, j, :],
                                                  S["mu_k"][:, j:j + 1],
                                                  uview[:, j, 0:TILE],
                                                  ALU.mult, ALU.add)
                return last
            fw.op("dve", mk, reads=["dbuf", "u", "mu_k"], writes=["xk"])

        return {"P0": P0, "P1": lq["P1"], "P2": lq["P2"],
                "P3": lq["P3"], "P4": lq["P4"], "C": lq["C"],
                "D1": lq["D1"], "D2": lq["D2"],
                "E": lambda: ln_bcast(True),
                "T1": T1, "T2": T2, "T3": T3, "T4": T4}

    def ffn_mm(wname, wres, rhsname, dlo=0, dhi=CH):
        def mm(e):
            last = None
            for dch in range(dlo, dhi):
                bank = PS[f"b{1 + dch // 2}"]
                cols = slice((dch % 2) * TILE, (dch % 2) * TILE + TILE)
                for j in range(CH):
                    last = e.matmul(bank[:, cols],
                                    S[wname][:, j * C + dch * 128: j * C + (dch + 1) * 128],
                                    S[rhsname][:, j * TILE:(j + 1) * TILE],
                                    start=(j == 0 and dch % 2 == 0), stop=(j == CH - 1))
            return last
        banks = [f"b{1 + d // 2}" for d in range(dlo, dhi, 2)]
        fw.op("pe", mm, reads=[wres, rhsname], writes=banks)

    BANKS = ["b1", "b2", "b3", "b4"]

    def ffn_B_slots(k, wnames):
        wr_n, wr_r, wkf_n, wkf_r, wvf_n, wvf_r = wnames
        c0 = k * TILE
        t2v = S["t2"][:].rearrange("p (c t) -> p c t", c=CH)

        def sig(half):
            def f(e, half=half):
                last = None
                for dch in range(4 * half, 4 * half + 4):
                    bank = PS[f"b{1 + dch // 2}"]
                    cols = slice((dch % 2) * TILE, (dch % 2) * TILE + TILE)
                    last = e.activation(S["rg"][:, dch * TILE:(dch + 1) * TILE],
                                        bank[:, cols], AF.Sigmoid)
                return last
            fw.op("act", f, reads=[f"b{1 + half * 2}", f"b{2 + half * 2}"],
                  writes=["rg"])

        def relusq(half):
            for b in range(1 + 2 * half, 3 + 2 * half):
                fw.op("act", (lambda e, b=b: e.activation(
                    S["treluW"][:], PS[f"b{b}"][:, 0:512], AF.Relu)),
                    reads=[f"b{b}"], writes=["treluW"])
                fw.op("act", (lambda e, b=b: e.activation(
                    S["kact"][:, (b - 1) * 512:b * 512], S["treluW"][:],
                    AF.Square)),
                    reads=["treluW"], writes=[f"kact{b}"])

        def vmm_j(j):
            def mm(e, j=j):
                last = None
                for dch in range(CH):
                    bank = PS[f"b{1 + dch // 2}"]
                    cols = slice((dch % 2) * TILE, (dch % 2) * TILE + TILE)
                    last = e.matmul(bank[:, cols],
                                    S[wvf_n][:, j * C + dch * 128: j * C + (dch + 1) * 128],
                                    S["kact"][:, j * TILE:(j + 1) * TILE],
                                    start=(j == 0 and dch % 2 == 0),
                                    stop=(j == CH - 1))
                return last
            fw.op("pe", mm, reads=[wvf_r, f"kact{1 + j // 2}"],
                  writes=["b1", "b2", "b3", "b4"])

        def rv():
            def f(e):
                last = None
                for dch in range(CH):
                    bank = PS[f"b{1 + dch // 2}"]
                    cols = slice((dch % 2) * TILE, (dch % 2) * TILE + TILE)
                    last = e.scalar_tensor_tensor(t2v[:, dch, :],
                                                  S["rg"][:, dch * TILE:(dch + 1) * TILE],
                                                  1.0, bank[:, cols], ALU.mult, ALU.mult)
                return last
            fw.op("dve", f, reads=["rg", "b1", "b2", "b3", "b4"], writes=["t2"])

        def acc():
            fw.op("dve", (lambda e: e.tensor_tensor(
                Xv[:, :, c0:c0 + TILE], Xv[:, :, c0:c0 + TILE], t2v, ALU.add)),
                reads=["t2", f"X{k}"], writes=[f"X{k}"])

        return {
            "rpreA": lambda: ffn_mm(wr_n, wr_r, "xr", 0, 4),
            "rpreB": lambda: ffn_mm(wr_n, wr_r, "xr", 4, CH),
            "sigA": lambda: sig(0),
            "sigB": lambda: sig(1),
            "kmmA": lambda: ffn_mm(wkf_n, wkf_r, "xk", 0, 4),
            "kmmB": lambda: ffn_mm(wkf_n, wkf_r, "xk", 4, CH),
            "relusqA": lambda: relusq(0),
            "relusqB": lambda: relusq(1),
            "vmm_j": vmm_j,
            "rv": rv,
            "acc": acc,
        }

    LNALL = ("P0", "P1", "P2", "P3", "P4", "C", "D1", "D2",
             "E", "T1", "T2", "T3", "T4")

    def ffn_phase(order, specials, wnames, emit_out):
        n = len(order)
        pc = [ffn_pieces(order[i], order[i] not in specials,
                         specials.get(order[i])) for i in range(n)]
        for nm in LNALL:
            pc[0][nm]()
        if n > 1:
            for nm in ("P0", "P1", "P2", "P3"):
                pc[1][nm]()
        for i, k in enumerate(order):
            B = ffn_B_slots(k, wnames)
            B["rpreA"]()
            if i + 1 < n:
                pc[i + 1]["P4"]()
                pc[i + 1]["C"]()
            B["sigA"]()
            B["rpreB"]()
            if i + 1 < n:
                pc[i + 1]["D1"]()
            B["sigB"]()
            B["kmmA"]()
            if i + 1 < n:
                pc[i + 1]["D2"]()
                pc[i + 1]["E"]()
                pc[i + 1]["T1"]()
            B["relusqA"]()
            B["kmmB"]()
            if i + 1 < n:
                pc[i + 1]["T2"]()
                pc[i + 1]["T3"]()
                pc[i + 1]["T4"]()
            B["relusqB"]()
            if i + 2 < n:
                pc[i + 2]["P0"]()
            for j in range(4):
                B["vmm_j"](j)
            if i + 2 < n:
                pc[i + 2]["P1"]()
            for j in range(4, CH):
                B["vmm_j"](j)
            if i + 2 < n:
                pc[i + 2]["P2"]()
            B["rv"]()
            if i + 2 < n:
                pc[i + 2]["P3"]()
            B["acc"]()
            if emit_out:
                fw.dma("sp", (lambda e, k=k: e.dma_start(
                    out=yT.rearrange("(j p) t -> p j t", p=128)[:, :, k * TILE:(k + 1) * TILE],
                    in_=Xv[:, :, k * TILE:(k + 1) * TILE])),
                    reads=[f"X{k}"], writes=[f"out{k}"])

    def finish_partial():
        for k in range(NT):
            fw.dma("sp", (lambda e, k=k: e.dma_start(
                out=yT.rearrange("(j p) t -> p j t", p=128)[:, :, k * TILE:(k + 1) * TILE],
                in_=Xv[:, :, k * TILE:(k + 1) * TILE])),
                reads=[f"X{k}"], writes=[f"out{k}"])
        with nc.Block() as block:
            fw.emit(block, stack)
        stack.close()
        return nc

    if upto == "pre":
        return finish_partial()

    ffn_phase(list(range(NT)), {0: "halo1"},
              ("wr", "Rwr", "wkf", "Rwkf", "wvf", "Rwvf"), emit_out=False)
    if upto == "ffn1":
        return finish_partial()

    for nm in ("kext0", "kext1", "vext0", "vext1"):
        fw.op("dve", (lambda e, nm=nm: e.memset(
            S[nm][:].rearrange("p (h w) -> p h w", w=65)[:, :, 64:65], 1.0)),
            writes=[nm])

    def attn_pieces(k):
        c0 = k * TILE
        u2 = "u2a" if k % 2 == 0 else "u2b"
        lq = ln_quarters(k)

        def AT1():
            fw.op("dve", lambda e: e.tensor_copy(S["mbsb"][:, 0:TILE],
                                                 PS["b0"][:, 0:TILE]),
                  reads=["b0"], writes=["mbsb"])
            fw.op("dve", (lambda e: e.tensor_tensor(
                S[u2][:].rearrange("p (c t) -> p c t", c=CH),
                Xv[:, :, c0:c0 + TILE],
                S["mbsb"][:, 0:TILE].unsqueeze(1).broadcast_to([128, CH, TILE]),
                ALU.subtract)),
                reads=[f"X{k}", "mbsb"], writes=[u2])

        def AT2():
            fw.dma("pool", (lambda e: e.dma_start(out=srow_d[0:1, :],
                                                  in_=S["rows"][0:1, TILE:2 * TILE])),
                   reads=["rows"], writes=["srow_d"])
            fw.dma("pool", (lambda e: e.dma_start(
                out=S["s_tok"][:, 2 * k:2 * k + 2],
                in_=srow_d.rearrange("o (i p) -> p (o i)", p=128),
                allow_slow_non_contiguous=True)),
                reads=["srow_d"], writes=["s_tok"])

        return {"P0": lq["P0"], "P1": lq["P1"], "P2": lq["P2"],
                "P3": lq["P3"], "P4": lq["P4"], "C": lq["C"],
                "D1": lq["D1"], "D2": lq["D2"],
                "E": lambda: ln_bcast(False),
                "AT1": AT1, "AT2": AT2}

    def qkv_mm(u2, sub, wname, wres, nb):
        def mm(e):
            last = None
            bank = PS[f"b{1 + nb}"]
            for j in range(CH):
                last = e.matmul(bank[:, 0:512],
                                S[u2][:, j * TILE + sub * 128: j * TILE + sub * 128 + 128],
                                S[wname][:, j * C + nb * 512: j * C + (nb + 1) * 512],
                                start=(j == 0), stop=(j == CH - 1))
            return last
        fw.op("pe", mm, reads=[u2, wres], writes=[f"b{1 + nb}"])

    def attn_B(k, hooks):
        u2 = "u2a" if k % 2 == 0 else "u2b"
        hook_i = [0]

        def hk():
            for f in hooks.get(hook_i[0], ()):
                f()
            hook_i[0] += 1

        for sub in range(2):
            g = 2 * k + sub
            ke = f"kext{g % 2}"
            ve = f"vext{g % 2}"
            def elu1(nb, fn):
                bank = f"b{1 + nb}"
                eb = "ebuf" if nb == 0 else "ebuf2"
                fw.op("act", (lambda e, bank=bank, g=g, eb=eb: e.activation(
                    S[eb][:], PS[bank][:, 0:512], AF.Exp,
                    scale=S["s_tok"][:, g:g + 1])),
                    reads=[bank, "s_tok"], writes=[eb])
                fw.op("act", (lambda e, bank=bank, g=g: e.activation(
                    S["rbuf"][:], PS[bank][:, 0:512], AF.Relu,
                    scale=S["s_tok"][:, g:g + 1])),
                    reads=[bank, "s_tok"], writes=["rbuf"])
                fn(nb, eb)

            def q_stt(nb, eb):
                fw.op("dve", (lambda e, nb=nb, eb=eb: e.scalar_tensor_tensor(
                    S["qtok"][:, nb * 512:(nb + 1) * 512], S[eb][:], 1.0,
                    S["rbuf"][:], ALU.min, ALU.add)),
                    reads=[eb, "rbuf"], writes=["qtok"])

            qkv_mm(u2, sub, "wq", "Rwq", 0)
            elu1(0, q_stt)
            qkv_mm(u2, sub, "wq", "Rwq", 1)
            hk()
            elu1(1, q_stt)

            def qtr(e):
                last = None
                for jj in range(CH):
                    last = e.transpose(PS["b7"][:, jj * 128:(jj + 1) * 128],
                                       S["qtok"][:, jj * 128:(jj + 1) * 128],
                                       S["ident_bf"][:])
                return last
            fw.op("pe", qtr, reads=["qtok", "ident_bf"], writes=["b7"])
            hk()
            fw.op("dve", (lambda e, g=g: e.tensor_copy(
                S["qfm"][:].rearrange("p (c t) -> p c t", c=CH)[:, :, g * 128:(g + 1) * 128],
                PS["b7"][:, :].rearrange("p (c t) -> p c t", c=CH))),
                reads=["b7"], writes=["qfm"])
            def k_stt(nb, eb):
                fw.op("dve", (lambda e, nb=nb, ke=ke, eb=eb: e.scalar_tensor_tensor(
                    S[ke][:, nb * 520:nb * 520 + 520].rearrange("p (h w) -> p h w", w=65)[:, :, 0:64],
                    S[eb][:].rearrange("p (h w) -> p h w", w=64), 1.0,
                    S["rbuf"][:].rearrange("p (h w) -> p h w", w=64), ALU.min, ALU.add)),
                    reads=[eb, "rbuf"], writes=[ke])

            qkv_mm(u2, sub, "wk", "Rwk", 0)
            elu1(0, k_stt)
            qkv_mm(u2, sub, "wk", "Rwk", 1)
            hk()
            elu1(1, k_stt)

            def v_mul(nb):
                bank = f"b{1 + nb}"
                fw.op("act", (lambda e, bank=bank, g=g, ve=ve, nb=nb: e.mul(
                    S[ve][:, nb * 520:nb * 520 + 520].rearrange("p (h w) -> p h w", w=65)[:, :, 0:64],
                    PS[bank][:, 0:512].rearrange("p (h w) -> p h w", w=64),
                    S["s_tok"][:, g:g + 1])),
                    reads=[bank, "s_tok"], writes=[ve])

            qkv_mm(u2, sub, "wv", "Rwv", 0)
            v_mul(0)
            qkv_mm(u2, sub, "wv", "Rwv", 1)
            hk()
            v_mul(1)

            def kvmm(e, ke=ke, ve=ve, g=g):
                last = None
                for h in range(H):
                    bi = min(h // 7, 2)
                    hh = h - 7 * bi
                    last = e.matmul(PS[f"b{4 + bi}"][0:65, hh * 65:(hh + 1) * 65],
                                    S[ke][:, h * 65:h * 65 + 65],
                                    S[ve][:, h * 65:h * 65 + 65],
                                    start=(g % NT == 0 and hh == 0),
                                    stop=(g % NT == NT - 1))
                return last
            fw.op("pe", kvmm, reads=[ke, ve], writes=["b4", "b5", "b6"])
            hk()
        hk()

    apc = [attn_pieces(k) for k in range(NT)]
    for nm in ("P0", "P1", "P2", "P3", "P4", "C", "D1", "D2", "E", "AT1", "AT2"):
        apc[0][nm]()
    for k in range(NT):
        if k + 1 < NT:
            p = apc[k + 1]
            hooks = {
                0: [p["P0"]],
                1: [p["P1"]],
                2: [p["P2"]],
                3: [p["P3"]],
                4: [p["P4"]],
                5: [p["C"]],
                6: [p["D1"], p["D2"]],
                7: [p["E"]],
                8: [p["AT1"]],
                9: [p["AT2"]],
            }
        else:
            hooks = {}
        attn_B(k, hooks)
        if k == 0:
            fw.op("pool", lambda e: e.nop(),
                  reads=["rg", "kact", "kact1", "kact2", "kact3", "kact4",
                         "dbuf", "t2"], writes=["wp_ok"])
            wload("wp", "wpT", "Rwp", extra_reads=["wp_ok"])
        if k == 3:
            fw.op("dve", (lambda e: (
                e.tensor_copy(S["kv_f"][0:65, 0:455], PS["b4"][0:65, 0:455]),
                e.tensor_copy(S["kv_f"][0:65, 455:910], PS["b5"][0:65, 0:455]),
                e.tensor_copy(S["kv_f"][0:65, 910:1040], PS["b6"][0:65, 0:130]))[-1]),
                reads=["b4", "b5", "b6"], writes=["kv_f"])
            fw.dma("sp", lambda e: e.dma_start(out=cc_kv_in[0:65, :],
                                               in_=S["kv_f"][0:65, :]),
                   reads=["kv_f"], writes=["cc_kv_in"])
            fw.cc(lambda e: e.collective_compute(
                "AllReduce", ALU.add, replica_groups=PAIRS,
                ins=[cc_kv_in.ap().opt()], outs=[cc_kv_out.ap().opt()]),
                reads=["cc_kv_in"], writes=["cc_kv_out"])

    def kvevac(e):
        e.tensor_copy(S["kv_f"][0:65, 0:455], PS["b4"][0:65, 0:455])
        e.tensor_copy(S["kv_f"][0:65, 455:910], PS["b5"][0:65, 0:455])
        return e.tensor_copy(S["kv_f"][0:65, 910:1040], PS["b6"][0:65, 0:130])
    fw.op("dve", kvevac, reads=["b4", "b5", "b6"], writes=["kv_f"])
    if upto == "attn1":
        return finish_partial()
    fw.dma("sp", lambda e: e.dma_start(out=cc_kv_in2[0:65, :], in_=S["kv_f"][0:65, :]),
           reads=["kv_f"], writes=["cc_kv_in2"])
    fw.cc(lambda e: e.collective_compute(
        "AllReduce", ALU.add, replica_groups=PAIRS,
        ins=[cc_kv_in2.ap().opt()], outs=[cc_kv_out2.ap().opt()]),
        reads=["cc_kv_in2"], writes=["cc_kv_out2"])
    fw.dma("sp", lambda e: e.dma_start(out=S["kv_f"][0:65, :], in_=cc_kv_out[0:65, :]),
           reads=["cc_kv_out", "kv_f"], writes=["kv_f"])
    fw.dma("sp", lambda e: e.dma_start(out=S["kvtmp"][0:65, :], in_=cc_kv_out2[0:65, :]),
           reads=["cc_kv_out2"], writes=["kvtmp"])
    fw.op("dve", (lambda e: e.tensor_tensor(
        S["kv_f"][0:65, :], S["kv_f"][0:65, :], S["kvtmp"][0:65, :], ALU.add)),
        reads=["kvtmp"], writes=["kv_f"])
    fw.op("dve", lambda e: e.memset(S["kv_bf"][:], 0.0), reads=["kv_f"], writes=["kv_bf"])
    fw.dma("pool", lambda e: e.dma_start(
        out=S["kv_bf"][0:64, 0:1024].rearrange("p (j w) -> p j w", w=128)[:, :, 0:64],
        in_=S["kv_f"][0:64, :].rearrange("p (j w) -> p j w", w=130)[:, :, 0:64]),
        reads=["kv_f"], writes=["kv_bf"])
    fw.dma("pool", lambda e: e.dma_start(
        out=S["kv_bf"][64:128, 0:1024].rearrange("p (j w) -> p j w", w=128)[:, :, 64:128],
        in_=S["kv_f"][0:64, :].rearrange("p (j w) -> p j w", w=130)[:, :, 65:129]),
        reads=["kv_f"], writes=["kv_bf"])
    fw.op("dve", lambda e: e.tensor_copy(
        S["kv_bf"][0:64, 1024:1040].rearrange("p (j w) -> p j w", w=2)[:, :, 0:1],
        S["kv_f"][0:64, :].rearrange("p (j w) -> p j w", w=130)[:, :, 64:65]),
        reads=["kv_f"], writes=["kv_bf"])
    fw.op("dve", lambda e: e.tensor_copy(
        S["kv_bf"][64:128, 1024:1040].rearrange("p (j w) -> p j w", w=2)[:, :, 1:2],
        S["kv_f"][0:64, :].rearrange("p (j w) -> p j w", w=130)[:, :, 129:130]),
        reads=["kv_f"], writes=["kv_bf"])


    fw.op("pool", lambda e: e.nop(), writes=["Rwq", "Rwk", "Rwv"])
    fw.op("pool", lambda e: e.nop(),
          reads=["Rwq", "Rwk", "Rwv"], writes=["reload_ok"])
    RELOADS = [("wr2", "wrT", "Rwr2"), ("wkf2", "wkfT", "Rwkf2"),
               ("wvf2", "wvfT", "Rwvf2")]

    if upto == "kv":
        return finish_partial()

    SUBBANKS = [("b4", "b5", 0), ("b3", "b0", 16)]

    def a2_pmm(k, sub):
        g = 2 * k + sub
        bk0, bk1, d0 = SUBBANKS[sub]

        def f(e):
            last = None
            for j in range(H // 2):
                bank = bk0 if j < 4 else bk1
                c = (j % 4) * 128
                lhsT = S["qfm"][:, j * TCORE + g * 128:j * TCORE + (g + 1) * 128]
                e.matmul(PS[bank][:, c:c + 128], lhsT,
                         S["kv_bf"][:, j * 128:(j + 1) * 128],
                         start=True, stop=True)
                last = e.matmul(PS["b6"][:, d0 + 2 * j:d0 + 2 * j + 2], lhsT,
                                S["kv_bf"][:, 1024 + 2 * j:1024 + 2 * j + 2],
                                start=True, stop=True)
            return last
        fw.op("pe", f, reads=["qfm", "kv_bf"], writes=[bk0, bk1, f"b6d{sub}"])

    def a2_chain(k, sub):
        bk0, bk1, d0 = SUBBANKS[sub]
        zd = "zden" if sub == 0 else "zden1"
        zr = "zrec" if sub == 0 else "zrec1"
        pt = "ptok" if sub == 0 else "qtok"

        fw.op("dve", (lambda e: e.tensor_copy(S[zd][:], PS["b6"][:, d0:d0 + 16])),
              reads=[f"b6d{sub}"], writes=[zd])
        fw.op("act", (lambda e: e.activation(S[zd][:], S[zd][:],
                                             AF.Relu, bias=S["cepsa"][:, 0:1])),
              reads=["cepsa"], writes=[zd])
        fw.op("dve", (lambda e: e.reciprocal(S[zr][:], S[zd][:])),
              reads=[zd], writes=[zr], fence=True)

        def sc(bk, z0, pcol):
            fw.op("dve", (lambda e: e.tensor_tensor(
                S[pt][:, pcol:pcol + 512].rearrange("p (h w) -> p h w", w=64),
                PS[bk][:, 0:512].rearrange("p (h w) -> p h w", w=64),
                S[zr][:, z0:z0 + 8].unsqueeze(2).broadcast_to([128, 8, 64]),
                ALU.mult)),
                reads=[bk, zr], writes=[pt])
        sc(bk0, 0, 0)
        sc(bk1, 8, 512)

    def a2_ptr(k, sub):
        pt = "ptok" if sub == 0 else "qtok"

        def ptr(e):
            last = None
            for jj in range(CH):
                last = e.transpose(PS["b7"][:, jj * 128:(jj + 1) * 128],
                                   S[pt][:, jj * 128:(jj + 1) * 128],
                                   S["ident_bf"][:])
            return last
        fw.op("pe", ptr, reads=[pt, "ident_bf"], writes=["b7"])
        fw.op("dve", (lambda e, sub=sub: e.tensor_copy(
            S["pfm"][:].rearrange("p (c t) -> p c t", c=CH)[:, :, sub * 128:(sub + 1) * 128],
            PS["b7"][:, :].rearrange("p (c t) -> p c t", c=CH))),
            reads=["b7"], writes=["pfm"])

    def a2_proj(k, half):
        def f(e):
            last = None
            for oc in range(4):
                och = half * 4 + oc
                bank = PS[f"b{1 + oc // 2}"]
                cols = slice((oc % 2) * TILE, (oc % 2) * TILE + TILE)
                for j in range(CH):
                    last = e.matmul(bank[:, cols],
                                    S["wp"][:, j * C + och * 128: j * C + (och + 1) * 128],
                                    S["pfm"][:, j * TILE:(j + 1) * TILE],
                                    start=(j == 0 and oc % 2 == 0), stop=(j == CH - 1))
            return last
        fw.op("pe", f, reads=["pfm", "Rwp"], writes=["b1", "b2"])

    def a2_evb(k, half):
        c0 = k * TILE

        def f(e):
            last = None
            for oc in range(4):
                och = half * 4 + oc
                bank = PS[f"b{1 + oc // 2}"]
                cols = slice((oc % 2) * TILE, (oc % 2) * TILE + TILE)
                last = e.scalar_tensor_tensor(
                    Xv[:, och, c0:c0 + TILE], Xv[:, och, c0:c0 + TILE],
                    S["bias"][:, och:och + 1], bank[:, cols],
                    ALU.add, ALU.add)
            return last
        fw.op("dve", f, reads=["b1", "b2", "bias", f"X{k}"], writes=[f"X{k}"])

    def a2_boundary():
        fw.op("act", (lambda e: e.mul(
            S["bndx"][:].unsqueeze(2), Xv[:, :, TCORE - 1:TCORE], 1.0)),
            reads=[f"X{NT - 1}"], writes=["bndx"], fence=True)
        fw.op("dve", lambda e: e.tensor_tensor(S["bndx"][:], S["bndx"][:],
                                               S["om_b"][:], ALU.mult),
              reads=["bndx", "om_b"], writes=["bndx"], fence=True)
        fw.dma("sp", lambda e: e.dma_start(out=cc_b_in[:, :], in_=S["bndx"][:]),
               reads=["bndx"], writes=["cc_b_in"])
        fw.cc(lambda e: e.collective_compute(
            "AllReduce", ALU.add, replica_groups=PAIRS,
            ins=[cc_b_in.ap().opt()], outs=[cc_b_out.ap().opt()]),
            reads=["cc_b_in"], writes=["cc_b_out"])
        fw.dma("sp", lambda e: e.dma_start(out=S["halo2w"][:, :], in_=cc_b_out[:, :]),
               reads=["cc_b_out"], writes=["halo2w"])
        fw.op("dve", lambda e: e.tensor_tensor(S["halo2"][:], S["halo2w"][:],
                                               S["mask_b"][:], ALU.mult),
              reads=["halo2w", "mask_b"], writes=["halo2"], fence=True)

    a2_order = [NT - 1] + list(range(NT - 1))
    a2_pmm(a2_order[0], 0)
    a2_pmm(a2_order[0], 1)
    a2_chain(a2_order[0], 0)
    a2_chain(a2_order[0], 1)
    for i, k in enumerate(a2_order):
        if a2lim and i >= a2lim:
            break
        a2_ptr(k, 0)
        a2_ptr(k, 1)
        kn = a2_order[i + 1] if i + 1 < NT else None
        if kn is not None:
            a2_pmm(kn, 0)
            a2_pmm(kn, 1)
        a2_proj(k, 0)
        a2_evb(k, 0)
        a2_proj(k, 1)
        a2_evb(k, 1)
        if kn is not None:
            a2_chain(kn, 0)
            a2_chain(kn, 1)
        if 1 <= i <= 6:
            dst, wsrc, res = RELOADS[(i - 1) // 2]
            half = (i - 1) % 2
            wload(dst, wsrc, res, extra_reads=["reload_ok"],
                  j0=half * 4, j1=half * 4 + 4, fin=(half == 1))
            if i == 6:
                fw.op("dve", (lambda e: e.tensor_scalar(
                    S["wvf2"][:], S["wvf2"][:], 0.5, None, ALU.mult)),
                    reads=["Rwvf2"], writes=["Rwvf2"])
        if k == NT - 1:
            a2_boundary()
        if k == 0:
            fw.op("act", (lambda e: e.mul(
                S["hsrcX"][:].unsqueeze(2), Xv[:, :, TILE - 1:TILE], 1.0)),
                reads=["X0"], writes=["hsrcX"], fence=True)

    if upto == "attn2":
        return finish_partial()

    ffn_phase(list(range(1, NT)) + [0], {1: "hsrcX", 0: "halo2"},
              ("wr2", "Rwr2", "wkf2", "Rwkf2", "wvf2", "Rwvf2"), emit_out=True)

    with nc.Block() as block:
        fw.emit(block, stack)
    stack.close()
    return nc


_NC_CACHE = None


def kernel(L0_attn_x, L0_attn_Wqkv, L0_attn_Wproj, L0_attn_bproj,
           L0_ffn_W_r, L0_ffn_W_k, L0_ffn_W_v, L0_ffn_mu_r, L0_ffn_mu_k):
    global _NC_CACHE
    x = np.asarray(L0_attn_x, dtype=np.float32)
    Wqkv = np.asarray(L0_attn_Wqkv, dtype=np.float32)
    Wproj = np.asarray(L0_attn_Wproj, dtype=np.float32)
    bproj = np.asarray(L0_attn_bproj, dtype=np.float32)
    W_r = np.asarray(L0_ffn_W_r, dtype=np.float32)
    W_k = np.asarray(L0_ffn_W_k, dtype=np.float32)
    W_v = np.asarray(L0_ffn_W_v, dtype=np.float32)
    mu_r = np.asarray(L0_ffn_mu_r, dtype=np.float32)
    mu_k = np.asarray(L0_ffn_mu_k, dtype=np.float32)

    B, N, _ = x.shape
    shared = {
        "wqT": np.ascontiguousarray(Wqkv[0:C].T),
        "wkT": np.ascontiguousarray(Wqkv[C:2 * C].T),
        "wvT": np.ascontiguousarray(Wqkv[2 * C:3 * C].T),
        "wpT": np.ascontiguousarray(Wproj.T),
        "wrT": np.ascontiguousarray(W_r.T),
        "wkfT": np.ascontiguousarray(W_k.T),
        "wvfT": np.ascontiguousarray(W_v.T),
        "mur8": np.ascontiguousarray(mu_r.reshape(CH, 128)),
        "muk8": np.ascontiguousarray(mu_k.reshape(CH, 128)),
        "bias8": np.ascontiguousarray(bproj.reshape(CH, 128)),
    }
    in_maps = []
    for core in range(8):
        b, h = core // 2, core % 2
        if h == 0:
            xs = np.concatenate([x[b, 0:1], x[b, 0:TCORE]], axis=0)
        else:
            xs = x[b, TCORE - 1:N]
        m = dict(shared)
        m["xT"] = np.ascontiguousarray(xs.T)
        m["mask8"] = np.full((128, CH), float(h), dtype=np.float32)
        m["om8"] = np.full((128, CH), float(1 - h), dtype=np.float32)
        in_maps.append(m)

    if _NC_CACHE is None:
        _NC_CACHE = build_program()
    res = run_bass_kernel_spmd(_NC_CACHE, in_maps, core_ids=list(range(8)))
    out = np.empty((B, N, C), dtype=np.float32)
    for core in range(8):
        b, h = core // 2, core % 2
        out[b, h * TCORE:(h + 1) * TCORE, :] = res.results[core]["yT"].T
    return out


# version 18 (final)
# speedup vs baseline: 1.0533x; 1.0533x over previous
import contextlib
import numpy as np

import concourse.bass as bass
import concourse.mybir as mybir
from concourse.bass_utils import run_bass_kernel_spmd

F32 = mybir.dt.float32
BF16 = mybir.dt.bfloat16
AF = mybir.ActivationFunctionType
ALU = mybir.AluOpType

C = 1024
CH = 8
TCORE = 2048
TILE = 256
NT = TCORE // TILE
H = 16
EPS_LN = 1e-5
EPS_ATT = 1e-6
PAIRS = [[0, 1], [2, 3], [4, 5], [6, 7]]

SBUF_BASE = 20608
SBUF_END = 229344

ALIAS = {
    "xr": ["u2a", "pfm"], "u2a": ["xr", "pfm"], "pfm": ["xr", "u2a"],
    "xk": ["u2b", "t2a_lo"], "u2b": ["xk", "t2a_lo"], "t2a_lo": ["xk", "u2b"],
    "rows": ["t2a_hi"], "mbsb": ["t2a_hi"], "t2a_hi": ["rows", "mbsb"],
    "u": ["ebuf", "rbuf", "qtok", "ptok", "ebuf2", "iota_j", "iota_p",
          "kvtmp"],
    "ebuf": ["u", "kvtmp"], "rbuf": ["u", "kvtmp"], "qtok": ["u", "kvtmp"],
    "ptok": ["u", "kvtmp", "ebuf2"],
    "iota_j": ["u", "kvtmp"], "iota_p": ["u", "kvtmp"],
    "kvtmp": ["u", "ebuf", "rbuf", "qtok", "ptok", "ebuf2",
              "iota_j", "iota_p"],
    "ebuf2": ["u", "ptok", "kvtmp"],
    "rg": ["Rwp"], "kact": ["Rwp"],
    "kact1": ["Rwp"], "kact2": ["Rwp"], "kact3": ["Rwp"], "kact4": ["Rwp"],
    "dbuf": ["t2", "Rwp"], "t2": ["dbuf", "Rwp"],
    "Rwp": ["rg", "kact", "kact1", "kact2", "kact3", "kact4", "dbuf", "t2"],
    "kext0": ["Rwvf"], "kext1": ["Rwvf"], "vext0": ["Rwvf"], "vext1": ["Rwvf"],
    "kv_f": ["Rwvf"],
    "Rwvf": ["kext0", "kext1", "vext0", "vext1", "kv_f"],
    "qfm": ["Rwr", "Rwkf"], "Rwr": ["qfm"], "Rwkf": ["qfm"],
    "Rwr2": ["Rwq"], "Rwq": ["Rwr2"],
    "Rwkf2": ["Rwk"], "Rwk": ["Rwkf2"],
    "Rwvf2": ["Rwv"], "Rwv": ["Rwvf2"],
    "b6": ["b6d0", "b6d1"], "b6d0": ["b6"], "b6d1": ["b6"],
}


class FW:

    def __init__(self, nc):
        self.nc = nc
        self.ops = []
        self.lastw = {}
        self.readers = {}
        self.n_cc = 0

    def op(self, eng, fn, reads=(), writes=(), kind="c", fence=False):
        wx = list(writes)
        for w in writes:
            wx += ALIAS.get(w, [])
        deps = set()
        for r in reads:
            if r in self.lastw:
                deps.add(self.lastw[r])
        for w in wx:
            if w in self.lastw:
                deps.add(self.lastw[w])
            deps.update(self.readers.get(w, ()))
        i = len(self.ops)
        ccid = None
        if kind == "cc":
            ccid = self.n_cc
            self.n_cc += 1
        self.ops.append(dict(eng=eng, kind=kind, fn=fn, deps=sorted(deps), ccid=ccid,
                             fence=fence))
        for r in reads:
            self.readers.setdefault(r, []).append(i)
        for w in writes:
            self.lastw[w] = i
            self.readers[w] = []
        return i

    def dma(self, eng, fn, reads=(), writes=()):
        return self.op(eng, fn, reads, writes, kind="d")

    def cc(self, fn, reads=(), writes=()):
        return self.op("pool", fn, reads, writes, kind="cc")

    def emit(self, block, stack):
        nc = self.nc
        sem_c = {e: stack.enter_context(nc.semaphore(f"sc_{e}"))
                 for e in ("pe", "act", "dve", "pool", "sp")}
        sem_d = {e: stack.enter_context(nc.semaphore(f"sd_{e}"))
                 for e in ("sp", "pool", "act")}
        cc_sems = [stack.enter_context(nc.semaphore(f"cc{i}")) for i in range(self.n_cc)]

        seq = {}
        counters = {}
        for i, o in enumerate(self.ops):
            if o["kind"] == "cc":
                continue
            key = (o["eng"], o["kind"])
            n = counters.get(key, 0)
            counters[key] = n + 1
            seq[i] = n

        def token(i):
            o = self.ops[i]
            if o["kind"] == "cc":
                return (cc_sems[o["ccid"]], 1)
            if o["kind"] == "c":
                return (sem_c[o["eng"]], seq[i] + 1)
            return (sem_d[o["eng"]], 16 * (seq[i] + 1))

        def emit_for(name):
            def run(eng):
                waited = {}
                for i, o in enumerate(self.ops):
                    if o["eng"] != name:
                        continue
                    want = {}
                    for d in o["deps"]:
                        p = self.ops[d]
                        if p["eng"] == name == "pe" and p["kind"] == "c":
                            continue
                        sem, val = token(d)
                        if want.get(sem, 0) < val:
                            want[sem] = val
                    for sem, val in sorted(want.items(), key=lambda kv: kv[0].num):
                        if waited.get(sem, 0) < val:
                            eng.wait_ge(sem, val)
                            waited[sem] = val
                    inst = o["fn"](eng)
                    if o["kind"] == "c":
                        inst.then_inc(sem_c[name], 1)
                    elif o["kind"] == "d":
                        inst.then_inc(sem_d[name], 16)
                    else:
                        inst.then_inc(cc_sems[o["ccid"]])
                if name == "sp":
                    for e in ("sp", "pool", "act"):
                        t = counters.get((e, "d"), 0)
                        if t:
                            eng.wait_ge(sem_d[e], 16 * t)
                    for s in cc_sems:
                        eng.wait_ge(s, 1)
            return run

        block.tensor(emit_for("pe"))
        block.scalar(emit_for("act"))
        block.vector(emit_for("dve"))
        block.gpsimd(emit_for("pool"))
        block.sync(emit_for("sp"))


def build_program(upto="full", dbg=False, no_bnd=False, a2lim=0):
    nc = bass.Bass("TRN2", target_bir_lowering=False, debug=False)

    xT = nc.dram_tensor("xT", [C, TCORE + 1], F32, kind="ExternalInput").ap()
    wdram = {}
    for name in ("wqT", "wkT", "wvT", "wpT", "wrT", "wkfT", "wvfT"):
        wdram[name] = nc.dram_tensor(name, [C, C], F32, kind="ExternalInput").ap()
    mur_in = nc.dram_tensor("mur8", [CH, 128], F32, kind="ExternalInput").ap()
    muk_in = nc.dram_tensor("muk8", [CH, 128], F32, kind="ExternalInput").ap()
    bias_in = nc.dram_tensor("bias8", [CH, 128], F32, kind="ExternalInput").ap()
    mask_in = nc.dram_tensor("mask8", [128, CH], F32, kind="ExternalInput").ap()
    om_in = nc.dram_tensor("om8", [128, CH], F32, kind="ExternalInput").ap()
    yT = nc.dram_tensor("yT", [C, TCORE], F32, kind="ExternalOutput").ap()
    cc_kv_in = nc.dram_tensor("cc_kv_in", [65, H * 65], F32)
    cc_kv_out = nc.dram_tensor("cc_kv_out", [65, H * 65], F32)
    cc_kv_in2 = nc.dram_tensor("cc_kv_in2", [65, H * 65], F32)
    cc_kv_out2 = nc.dram_tensor("cc_kv_out2", [65, H * 65], F32)
    cc_b_in = nc.dram_tensor("cc_b_in", [128, CH], F32)
    cc_b_out = nc.dram_tensor("cc_b_out", [128, CH], F32)
    srow_d = nc.dram_tensor("srow_d", [1, TILE], F32)
    if dbg:
        dbg_o = nc.dram_tensor("dbg", [128, 96], F32, kind="ExternalOutput").ap()

    S = {}

    def at(name, shape, dtype, off):
        assert SBUF_BASE <= off and off + shape[1] * mybir.dt.size(dtype) <= SBUF_END, name
        S[name] = nc.alloc_sbuf_tensor_at(name, shape, dtype, offset=off).ap()
        return off

    cur = SBUF_BASE

    def alloc(name, shape, dtype):
        nonlocal cur
        cur = (cur + 31) // 32 * 32
        off = at(name, shape, dtype, cur)
        cur += shape[1] * mybir.dt.size(dtype)
        return off

    alloc("X", [128, CH * TCORE], F32)
    off_wr = alloc("wr", [128, CH * C], BF16)
    at("qfm", [128, CH * TCORE], BF16, off_wr)
    alloc("wkf", [128, CH * C], BF16)
    off_wvf = alloc("wvf", [128, CH * C], BF16)
    at("kext0", [128, 1040], BF16, off_wvf)
    at("kext1", [128, 1040], BF16, off_wvf + 2080)
    at("vext0", [128, 1040], BF16, off_wvf + 4160)
    at("vext1", [128, 1040], BF16, off_wvf + 6240)
    at("kv_f", [128, H * 65], F32, off_wvf + 8320)
    off_wq = alloc("wq", [128, CH * C], BF16)

    at("wr2", [128, CH * C], BF16, off_wq)
    off_wk = alloc("wk", [128, CH * C], BF16)
    at("wkf2", [128, CH * C], BF16, off_wk)
    off_wv = alloc("wv", [128, CH * C], BF16)
    at("wvf2", [128, CH * C], BF16, off_wv)
    off_wp = alloc("wp", [128, CH * C], BF16)
    at("rg", [128, CH * TILE], BF16, off_wp)
    at("kact", [128, CH * TILE], BF16, off_wp + 4096)
    off_big = off_wp + 8192
    at("dbuf", [128, CH * TILE], F32, off_big)
    at("t2", [128, CH * TILE], F32, off_big)

    alloc("ident_bf", [128, 128], BF16)
    alloc("ones_inv", [128, 1], BF16)
    alloc("ones_row", [1, 128], F32)
    alloc("mask_b", [128, CH], F32)
    alloc("om_b", [128, CH], F32)
    alloc("mu_r", [128, CH], F32)
    alloc("mu_k", [128, CH], F32)
    alloc("bias", [128, CH], F32)
    alloc("uhalo0", [128, CH], F32)
    alloc("uhalo1", [128, CH], F32)
    alloc("halo1", [128, CH], F32)
    alloc("halo2w", [128, CH], F32)
    alloc("halo2", [128, CH], F32)
    alloc("bndx", [128, CH], F32)
    alloc("hsrc_bf", [128, CH], BF16)
    alloc("hsrcX", [128, CH], F32)
    alloc("s_tok", [128, 2 * NT], F32)
    off_kvbf = alloc("kv_bf", [128, H * 65], BF16)
    alloc("zden", [128, H], F32)
    alloc("zrec", [128, H], F32)
    alloc("zden1", [128, H], F32)
    alloc("zrec1", [128, H], F32)
    alloc("hrow", [1, 8], F32)
    alloc("hmb", [128, 2], F32)
    alloc("uhsp", [128, CH], F32)
    alloc("ceps", [128, 1], F32)
    alloc("cepsa", [128, 1], F32)


    alloc("x_bf", [128, 4 * TILE], BF16)
    off_rows = alloc("rows", [1, 2 * TILE], F32)
    at("mbsb", [128, 2 * TILE], F32, off_rows + 2048)
    cur += 2048
    at("t2a_hi", [128, 4 * TILE], F32, off_rows)
    off_u = alloc("u", [128, CH * (TILE + 1)], F32)
    at("ebuf", [128, 512], F32, off_u)
    at("rbuf", [128, 512], F32, off_u + 2048)
    at("qtok", [128, 1024], BF16, off_u + 4096)
    at("ptok", [128, 1024], BF16, off_u + 6144)
    at("ebuf2", [128, 512], F32, off_u + 6144)
    at("iota_j", [128, 128], F32, off_u)
    at("iota_p", [128, 1], F32, off_u + 512)
    at("kvtmp", [65, H * 65], F32, off_u)
    at("m2row", [1, TILE], F32, off_u)
    at("sxrow", [1, TILE], F32, off_u + 1024)
    off_xr = alloc("xr", [128, CH * TILE], BF16)
    at("u2a", [128, CH * TILE], BF16, off_xr)
    at("pfm", [128, CH * TILE], BF16, off_xr)
    off_xk = alloc("xk", [128, CH * TILE], BF16)
    at("u2b", [128, CH * TILE], BF16, off_xk)
    at("t2a_lo", [128, 4 * TILE], F32, off_xk)
    off_trelu = alloc("trelu0", [128, TILE], F32)
    alloc("trelu1", [128, TILE], F32)
    at("treluW", [128, 2 * TILE], F32, off_trelu)
    assert cur <= SBUF_END, f"SBUF overflow: {cur} > {SBUF_END}"

    stack = contextlib.ExitStack()
    PS = {}
    for i in range(7):
        PS[f"b{i}"] = stack.enter_context(nc.psum_tensor(f"psb{i}", [128, 512], F32))
    PS["b7"] = stack.enter_context(nc.psum_tensor("psb7", [128, 1024], BF16))

    fw = FW(nc)
    Xv = S["X"][:].rearrange("p (c t) -> p c t", c=CH)

    def tap(c0, view, reads):
        if dbg:
            fw.dma("sp", (lambda e: e.dma_start(out=dbg_o[:, c0:c0 + view.shape[-1] if view.ndim == 2 else c0 + view.shape[1]], in_=view)
                   if view.ndim == 2 else None), reads=reads, writes=[f"dbgt{c0}"])

    def tap2(c0, mk, reads, q="sp"):
        if dbg:
            fw.dma(q, mk, reads=reads, writes=[f"dbgt{c0}"])

    for j in range(CH):
        fw.dma("sp", (lambda e, j=j: e.dma_start(
            out=S["X"][:, j * TCORE:j * TCORE + TILE],
            in_=xT[j * 128:(j + 1) * 128, 1:TILE + 1])),
            writes=[f"Xla{j}"])
    fw.op("sp", lambda e: e.nop(),
          reads=[f"Xla{j}" for j in range(CH)], writes=["X0"])
    for j in range(CH):
        fw.dma("sp", (lambda e, j=j: e.dma_start(
            out=S["X"][:, j * TCORE + TILE:(j + 1) * TCORE],
            in_=xT[j * 128:(j + 1) * 128, TILE + 1:TCORE + 1])),
            writes=[f"Xlb{j}"])
    fw.op("sp", lambda e: e.nop(),
          reads=[f"Xlb{j}" for j in range(CH)],
          writes=[f"X{k}" for k in range(1, NT)])

    fw.dma("sp", lambda e: e.dma_start(
        out=S["halo1"][:], in_=xT[:, 0:1].rearrange("(j p) o -> p (j o)", p=128),
        allow_slow_non_contiguous=True), writes=["halo1"])
    fw.dma("sp", lambda e: e.dma_start(out=S["mask_b"][:], in_=mask_in),
           writes=["mask_b"])
    fw.dma("sp", lambda e: e.dma_start(out=S["om_b"][:], in_=om_in),
           writes=["om_b"])
    fw.dma("sp", lambda e: e.dma_start(out=S["mu_r"][:], in_=mur_in.rearrange("j p -> p j"),
                                       allow_slow_non_contiguous=True), writes=["mu_r"])
    fw.dma("sp", lambda e: e.dma_start(out=S["mu_k"][:], in_=muk_in.rearrange("j p -> p j"),
                                       allow_slow_non_contiguous=True), writes=["mu_k"])
    fw.dma("sp", lambda e: e.dma_start(out=S["bias"][:], in_=bias_in.rearrange("j p -> p j"),
                                       allow_slow_non_contiguous=True), writes=["bias"])

    fw.op("pool", lambda e: e.iota(S["iota_j"][:], pattern=[[1, 128]], base=0,
                                   channel_multiplier=0, allow_small_or_imprecise_dtypes=True),
          writes=["iota_j"])
    fw.op("pool", lambda e: e.iota(S["iota_p"][:], pattern=[[0, 1]], base=0,
                                   channel_multiplier=1, allow_small_or_imprecise_dtypes=True),
          writes=["iota_p"])
    fw.op("dve", lambda e: e.tensor_scalar(S["ident_bf"][:], S["iota_j"][:], S["iota_p"][:],
                                           None, ALU.is_equal),
          reads=["iota_j", "iota_p"], writes=["ident_bf"])
    fw.op("dve", lambda e: e.memset(S["ones_inv"][:], 1.0 / C), writes=["ones_inv"])
    fw.op("dve", lambda e: e.memset(S["ones_row"][0:1, :], 1.0), writes=["ones_row"])
    fw.op("dve", lambda e: e.memset(S["ceps"][:], EPS_LN), writes=["ceps"])
    fw.op("dve", lambda e: e.memset(S["cepsa"][:], EPS_ATT), writes=["cepsa"])
    fw.op("dve", lambda e: e.tensor_tensor(S["halo1"][:], S["halo1"][:],
                                           S["mask_b"][:], ALU.mult),
          reads=["mask_b", "halo1"], writes=["halo1"], fence=True)

    def wload(dst, src, res, extra_reads=(), j0=0, j1=CH, fin=True):
        for j in range(j0, j1):
            fw.dma("pool", (lambda e, j=j: e.dma_start(
                out=S[dst][:, j * C:(j + 1) * C],
                in_=wdram[src][j * 128:(j + 1) * 128, :])),
                reads=list(extra_reads), writes=[f"{res}_{j}"])
        if fin:
            fw.op("pool", lambda e: e.nop(),
                  reads=[f"{res}_{j}" for j in range(CH)], writes=[res])

    wload("wr", "wrT", "Rwr")
    wload("wkf", "wkfT", "Rwkf")
    wload("wvf", "wvfT", "Rwvf")
    fw.op("dve", lambda e: e.tensor_scalar(S["wvf"][:], S["wvf"][:], 0.5, None, ALU.mult),
          reads=["Rwvf"], writes=["Rwvf"])
    wload("wq", "wqT", "Rwq")
    wload("wk", "wkT", "Rwk")
    wload("wv", "wvT", "Rwv")

    def ln_quarters(k, fake_src=None):
        c0 = k * TILE

        def castq(q):
            h = q % 2
            dst = S["x_bf"][:, h * 512:h * 512 + 512].rearrange(
                "p (c t) -> p c t", c=2)
            if fake_src is None:
                fw.dma("pool", (lambda e, q=q, dst=dst: e.dma_start(
                    out=dst, in_=Xv[:, 2 * q:2 * q + 2, c0:c0 + TILE])),
                    reads=[f"X{k}"], writes=[f"x_bf{h}"])
            else:
                fw.op("dve", (lambda e, q=q, dst=dst: e.tensor_copy(
                    dst, S[fake_src][:, 2 * q:2 * q + 2].unsqueeze(2)
                    .broadcast_to([128, 2, TILE]))),
                    reads=[fake_src], writes=[f"x_bf{h}"])

        def summsq(q):
            h = q % 2

            def f(e, h=h, q=q):
                last = None
                for jj in range(2):
                    last = e.matmul(PS["b0"][0:1, 0:TILE], S["ones_inv"][:, 0:1],
                                    S["x_bf"][:, h * 512 + jj * TILE:
                                               h * 512 + (jj + 1) * TILE],
                                    start=(q == 0 and jj == 0),
                                    stop=(q == 3 and jj == 1))
                return last
            fw.op("pe", f, reads=[f"x_bf{h}"], writes=["b0"])

        def sqq(q):
            h = q % 2
            v = S["x_bf"][:, h * 512:h * 512 + 512]
            fw.op("act", (lambda e, v=v: e.activation(v, v, AF.Square)),
                  reads=[f"x_bf{h}"], writes=[f"x_bf{h}"])

        def sqmmsq(q):
            h = q % 2

            def f(e, h=h, q=q):
                last = None
                for jj in range(2):
                    last = e.matmul(PS["b0"][0:1, TILE:2 * TILE],
                                    S["ones_inv"][:, 0:1],
                                    S["x_bf"][:, h * 512 + jj * TILE:
                                               h * 512 + (jj + 1) * TILE],
                                    start=False, stop=(q == 3 and jj == 1))
                return last
            fw.op("pe", f, reads=[f"x_bf{h}"], writes=["b0"])

        def rowblob():
            sqmmsq(2)
            sqmmsq(3)
            fw.op("dve", lambda e: e.tensor_copy(S["rows"][0:1, :],
                                                 PS["b0"][0:1, 0:2 * TILE]),
                  reads=["b0"], writes=["rows"])
            fw.op("dve", (lambda e: e.tensor_tensor(
                S["x_bf"][0:1, 0:TILE], S["rows"][0:1, 0:TILE],
                S["rows"][0:1, 0:TILE], ALU.mult)),
                reads=["rows"], writes=["x_bf0"])
            fw.op("dve", (lambda e: e.tensor_tensor(
                S["rows"][0:1, TILE:2 * TILE], S["rows"][0:1, TILE:2 * TILE],
                S["x_bf"][0:1, 0:TILE], ALU.subtract)),
                reads=["x_bf0"], writes=["rows"])

        def rsqrt1():
            fw.op("act", lambda e: e.activation(S["rows"][0:1, TILE:2 * TILE],
                                                S["rows"][0:1, TILE:2 * TILE],
                                                AF.Sqrt, bias=S["ceps"][0:1, :]),
                  reads=["ceps"], writes=["rows"])

        def rsqrt2():
            fw.op("dve", lambda e: e.reciprocal(S["rows"][0:1, TILE:2 * TILE],
                                                S["rows"][0:1, TILE:2 * TILE]),
                  reads=[], writes=["rows"])

        return {
            "P0": lambda: (castq(0), castq(1)),
            "P1": lambda: (summsq(0), sqq(0)),
            "P2": lambda: (summsq(1), sqq(1), sqmmsq(0), castq(2)),
            "P3": lambda: (summsq(2), sqq(2), sqmmsq(1), castq(3)),
            "P4": lambda: (summsq(3), sqq(3)),
            "C": rowblob,
            "D1": rsqrt1, "D2": rsqrt2,
        }

    def ln_bcast(both):
        def f(e):
            r = e.matmul(PS["b0"][:, 0:TILE], S["ones_row"][0:1, :],
                         S["rows"][0:1, 0:TILE], start=True, stop=True)
            if both:
                r = e.matmul(PS["b0"][:, TILE:2 * TILE], S["ones_row"][0:1, :],
                             S["rows"][0:1, TILE:2 * TILE],
                             start=False, stop=True)
            return r
        fw.op("pe", f, reads=["rows", "ones_row"], writes=["b0"])

    uview = S["u"][:].rearrange("p (c t) -> p c t", c=CH)

    def halo_special(hsrc):
        lq = ln_quarters(0, fake_src=hsrc)
        for nm in ("P0", "P1", "P2", "P3", "P4", "C", "D1", "D2"):
            lq[nm]()
        ln_bcast(True)
        fw.op("dve", lambda e: e.tensor_copy(S["mbsb"][:, :], PS["b0"][:, :]),
              reads=["b0"], writes=["mbsb"])
        fw.op("dve", (lambda e: e.tensor_tensor(
            S["uhsp"][:], S[hsrc][:],
            S["mbsb"][:, 0:1].broadcast_to([128, CH]), ALU.subtract)),
            reads=[hsrc, "mbsb"], writes=["uhsp"])
        fw.op("dve", (lambda e: e.tensor_tensor(
            S["uhsp"][:], S["uhsp"][:],
            S["mbsb"][:, TILE:TILE + 1].broadcast_to([128, CH]), ALU.mult)),
            reads=["mbsb"], writes=["uhsp"], fence=True)

    def ffn_pieces(k, chain, hsrc):
        c0 = k * TILE
        dv = S["dbuf"][:].rearrange("p (c t) -> p c t", c=CH)
        xrv = S["xr"][:].rearrange("p (c t) -> p c t", c=CH)
        xkv = S["xk"][:].rearrange("p (c t) -> p c t", c=CH)
        lq = ln_quarters(k)

        def P0():
            if not chain:
                halo_special(hsrc)
            lq["P0"]()

        def T1():
            fw.op("dve", lambda e: e.tensor_copy(S["mbsb"][:, :], PS["b0"][:, :]),
                  reads=["b0"], writes=["mbsb"])

        def T2():
            fw.op("dve", (lambda e: e.tensor_tensor(
                uview[:, :, 1:TILE + 1], Xv[:, :, c0:c0 + TILE],
                S["mbsb"][:, 0:TILE].unsqueeze(1).broadcast_to([128, CH, TILE]),
                ALU.subtract)),
                reads=[f"X{k}", "mbsb"], writes=["u"])
            fw.op("dve", (lambda e: e.tensor_tensor(
                uview[:, :, 1:TILE + 1], uview[:, :, 1:TILE + 1],
                S["mbsb"][:, TILE:2 * TILE].unsqueeze(1).broadcast_to([128, CH, TILE]),
                ALU.mult)),
                reads=["mbsb"], writes=["u"])

        def T3():
            uh_r = f"uhalo{(k - 1) % 2}" if chain else "uhsp"
            fw.op("dve", (lambda e, uh_r=uh_r: e.tensor_copy(
                uview[:, :, 0:1], S[uh_r][:].unsqueeze(2))),
                reads=[uh_r], writes=["u"], fence=True)
            uh_w = f"uhalo{k % 2}"
            fw.op("dve", (lambda e, uh_w=uh_w: e.tensor_copy(
                S[uh_w][:].unsqueeze(2), uview[:, :, TILE:TILE + 1])),
                reads=["u"], writes=[uh_w], fence=True)
            fw.op("dve", (lambda e: e.tensor_tensor(
                dv, uview[:, :, 1:TILE + 1], uview[:, :, 0:TILE], ALU.subtract)),
                reads=["u"], writes=["dbuf"])

        def T4():
            def mr(e):
                last = None
                for j in range(CH):
                    last = e.scalar_tensor_tensor(xrv[:, j, :], dv[:, j, :],
                                                  S["mu_r"][:, j:j + 1],
                                                  uview[:, j, 0:TILE],
                                                  ALU.mult, ALU.add)
                return last
            fw.op("dve", mr, reads=["dbuf", "u", "mu_r"], writes=["xr"])

            def mk(e):
                last = None
                for j in range(CH):
                    last = e.scalar_tensor_tensor(xkv[:, j, :], dv[:, j, :],
                                                  S["mu_k"][:, j:j + 1],
                                                  uview[:, j, 0:TILE],
                                                  ALU.mult, ALU.add)
                return last
            fw.op("dve", mk, reads=["dbuf", "u", "mu_k"], writes=["xk"])

        return {"P0": P0, "P1": lq["P1"], "P2": lq["P2"],
                "P3": lq["P3"], "P4": lq["P4"], "C": lq["C"],
                "D1": lq["D1"], "D2": lq["D2"],
                "E": lambda: ln_bcast(True),
                "T1": T1, "T2": T2, "T3": T3, "T4": T4}

    def ffn_mm(wname, wres, rhsname, dlo=0, dhi=CH):
        def mm(e):
            last = None
            for dch in range(dlo, dhi):
                bank = PS[f"b{1 + dch // 2}"]
                cols = slice((dch % 2) * TILE, (dch % 2) * TILE + TILE)
                for j in range(CH):
                    last = e.matmul(bank[:, cols],
                                    S[wname][:, j * C + dch * 128: j * C + (dch + 1) * 128],
                                    S[rhsname][:, j * TILE:(j + 1) * TILE],
                                    start=(j == 0 and dch % 2 == 0), stop=(j == CH - 1))
            return last
        banks = [f"b{1 + d // 2}" for d in range(dlo, dhi, 2)]
        fw.op("pe", mm, reads=[wres, rhsname], writes=banks)

    BANKS = ["b1", "b2", "b3", "b4"]

    def ffn_B_slots(k, wnames):
        wr_n, wr_r, wkf_n, wkf_r, wvf_n, wvf_r = wnames
        c0 = k * TILE
        t2v = S["t2"][:].rearrange("p (c t) -> p c t", c=CH)

        def sig(half):
            def f(e, half=half):
                last = None
                for dch in range(4 * half, 4 * half + 4):
                    bank = PS[f"b{1 + dch // 2}"]
                    cols = slice((dch % 2) * TILE, (dch % 2) * TILE + TILE)
                    last = e.activation(S["rg"][:, dch * TILE:(dch + 1) * TILE],
                                        bank[:, cols], AF.Sigmoid)
                return last
            fw.op("act", f, reads=[f"b{1 + half * 2}", f"b{2 + half * 2}"],
                  writes=["rg"])

        def relusq(half):
            for b in range(1 + 2 * half, 3 + 2 * half):
                fw.op("act", (lambda e, b=b: e.activation(
                    S["treluW"][:], PS[f"b{b}"][:, 0:512], AF.Relu)),
                    reads=[f"b{b}"], writes=["treluW"])
                fw.op("act", (lambda e, b=b: e.activation(
                    S["kact"][:, (b - 1) * 512:b * 512], S["treluW"][:],
                    AF.Square)),
                    reads=["treluW"], writes=[f"kact{b}"])

        def vmm_j(j):
            def mm(e, j=j):
                last = None
                for dch in range(CH):
                    bank = PS[f"b{1 + dch // 2}"]
                    cols = slice((dch % 2) * TILE, (dch % 2) * TILE + TILE)
                    last = e.matmul(bank[:, cols],
                                    S[wvf_n][:, j * C + dch * 128: j * C + (dch + 1) * 128],
                                    S["kact"][:, j * TILE:(j + 1) * TILE],
                                    start=(j == 0 and dch % 2 == 0),
                                    stop=(j == CH - 1))
                return last
            fw.op("pe", mm, reads=[wvf_r, f"kact{1 + j // 2}"],
                  writes=["b1", "b2", "b3", "b4"])

        def rv(half):
            def f(e, half=half):
                last = None
                for dch in range(4 * half, 4 * half + 4):
                    bank = PS[f"b{1 + dch // 2}"]
                    cols = slice((dch % 2) * TILE, (dch % 2) * TILE + TILE)
                    last = e.scalar_tensor_tensor(t2v[:, dch, :],
                                                  S["rg"][:, dch * TILE:(dch + 1) * TILE],
                                                  1.0, bank[:, cols], ALU.mult, ALU.mult)
                return last
            fw.op("dve", f, reads=["rg", f"b{1 + 2 * half}", f"b{2 + 2 * half}"],
                  writes=["t2"])

        def acc():
            fw.op("dve", (lambda e: e.tensor_tensor(
                Xv[:, :, c0:c0 + TILE], Xv[:, :, c0:c0 + TILE], t2v, ALU.add)),
                reads=["t2", f"X{k}"], writes=[f"X{k}"])

        return {
            "rpreA": lambda: ffn_mm(wr_n, wr_r, "xr", 0, 4),
            "rpreB": lambda: ffn_mm(wr_n, wr_r, "xr", 4, CH),
            "sigA": lambda: sig(0),
            "sigB": lambda: sig(1),
            "kmmA": lambda: ffn_mm(wkf_n, wkf_r, "xk", 0, 4),
            "kmmB": lambda: ffn_mm(wkf_n, wkf_r, "xk", 4, CH),
            "relusqA": lambda: relusq(0),
            "relusqB": lambda: relusq(1),
            "vmm_j": vmm_j,
            "rvA": lambda: rv(0),
            "rvB": lambda: rv(1),
            "acc": acc,
        }

    LNALL = ("P0", "P1", "P2", "P3", "P4", "C", "D1", "D2",
             "E", "T1", "T2", "T3", "T4")

    def ffn_phase(order, specials, wnames, emit_out):
        n = len(order)
        pc = [ffn_pieces(order[i], order[i] not in specials,
                         specials.get(order[i])) for i in range(n)]
        for nm in LNALL:
            pc[0][nm]()
        if n > 1:
            for nm in ("P0", "P1", "P2", "P3"):
                pc[1][nm]()
        for i, k in enumerate(order):
            B = ffn_B_slots(k, wnames)
            B["rpreA"]()
            if i + 1 < n:
                pc[i + 1]["P4"]()
                pc[i + 1]["C"]()
            B["sigA"]()
            B["rpreB"]()
            if i + 1 < n:
                pc[i + 1]["D1"]()
            B["sigB"]()
            B["kmmA"]()
            if i + 1 < n:
                pc[i + 1]["D2"]()
                pc[i + 1]["E"]()
                pc[i + 1]["T1"]()
            B["relusqA"]()
            B["kmmB"]()
            if i + 1 < n:
                pc[i + 1]["T2"]()
                pc[i + 1]["T3"]()
                pc[i + 1]["T4"]()
            B["relusqB"]()
            if i + 2 < n:
                pc[i + 2]["P0"]()
            for j in range(4):
                B["vmm_j"](j)
            if i + 2 < n:
                pc[i + 2]["P1"]()
            for j in range(4, CH):
                B["vmm_j"](j)
            if i + 2 < n:
                pc[i + 2]["P2"]()
            B["rvA"]()
            B["rvB"]()
            if i + 2 < n:
                pc[i + 2]["P3"]()
            B["acc"]()
            if emit_out:
                fw.dma("sp", (lambda e, k=k: e.dma_start(
                    out=yT.rearrange("(j p) t -> p j t", p=128)[:, :, k * TILE:(k + 1) * TILE],
                    in_=Xv[:, :, k * TILE:(k + 1) * TILE])),
                    reads=[f"X{k}"], writes=[f"out{k}"])

    def finish_partial():
        for k in range(NT):
            fw.dma("sp", (lambda e, k=k: e.dma_start(
                out=yT.rearrange("(j p) t -> p j t", p=128)[:, :, k * TILE:(k + 1) * TILE],
                in_=Xv[:, :, k * TILE:(k + 1) * TILE])),
                reads=[f"X{k}"], writes=[f"out{k}"])
        with nc.Block() as block:
            fw.emit(block, stack)
        stack.close()
        return nc

    if upto == "pre":
        return finish_partial()

    ffn_phase(list(range(NT)), {0: "halo1"},
              ("wr", "Rwr", "wkf", "Rwkf", "wvf", "Rwvf"), emit_out=False)
    if upto == "ffn1":
        return finish_partial()

    for nm in ("kext0", "kext1", "vext0", "vext1"):
        fw.op("dve", (lambda e, nm=nm: e.memset(
            S[nm][:].rearrange("p (h w) -> p h w", w=65)[:, :, 64:65], 1.0)),
            writes=[nm])

    def attn_pieces(k):
        c0 = k * TILE
        u2 = "u2a" if k % 2 == 0 else "u2b"
        lq = ln_quarters(k)

        def AT1():
            fw.op("dve", lambda e: e.tensor_copy(S["mbsb"][:, 0:TILE],
                                                 PS["b0"][:, 0:TILE]),
                  reads=["b0"], writes=["mbsb"])
            fw.op("dve", (lambda e: e.tensor_tensor(
                S[u2][:].rearrange("p (c t) -> p c t", c=CH),
                Xv[:, :, c0:c0 + TILE],
                S["mbsb"][:, 0:TILE].unsqueeze(1).broadcast_to([128, CH, TILE]),
                ALU.subtract)),
                reads=[f"X{k}", "mbsb"], writes=[u2])

        def AT2():
            fw.dma("pool", (lambda e: e.dma_start(out=srow_d[0:1, :],
                                                  in_=S["rows"][0:1, TILE:2 * TILE])),
                   reads=["rows"], writes=["srow_d"])
            fw.dma("pool", (lambda e: e.dma_start(
                out=S["s_tok"][:, 2 * k:2 * k + 2],
                in_=srow_d.rearrange("o (i p) -> p (o i)", p=128),
                allow_slow_non_contiguous=True)),
                reads=["srow_d"], writes=["s_tok"])

        return {"P0": lq["P0"], "P1": lq["P1"], "P2": lq["P2"],
                "P3": lq["P3"], "P4": lq["P4"], "C": lq["C"],
                "D1": lq["D1"], "D2": lq["D2"],
                "E": lambda: ln_bcast(False),
                "AT1": AT1, "AT2": AT2}

    def qkv_mm(u2, sub, wname, wres, nb):
        def mm(e):
            last = None
            bank = PS[f"b{1 + nb}"]
            for j in range(CH):
                last = e.matmul(bank[:, 0:512],
                                S[u2][:, j * TILE + sub * 128: j * TILE + sub * 128 + 128],
                                S[wname][:, j * C + nb * 512: j * C + (nb + 1) * 512],
                                start=(j == 0), stop=(j == CH - 1))
            return last
        fw.op("pe", mm, reads=[u2, wres], writes=[f"b{1 + nb}"])

    def attn_B(k, hooks):
        u2 = "u2a" if k % 2 == 0 else "u2b"
        hook_i = [0]

        def hk():
            for f in hooks.get(hook_i[0], ()):
                f()
            hook_i[0] += 1

        for sub in range(2):
            g = 2 * k + sub
            ke = f"kext{g % 2}"
            ve = f"vext{g % 2}"
            def elu1(nb, fn):
                bank = f"b{1 + nb}"
                eb = "ebuf" if nb == 0 else "ebuf2"
                fw.op("act", (lambda e, bank=bank, g=g, eb=eb: e.activation(
                    S[eb][:], PS[bank][:, 0:512], AF.Exp,
                    scale=S["s_tok"][:, g:g + 1])),
                    reads=[bank, "s_tok"], writes=[eb])
                fw.op("act", (lambda e, bank=bank, g=g: e.activation(
                    S["rbuf"][:], PS[bank][:, 0:512], AF.Relu,
                    scale=S["s_tok"][:, g:g + 1])),
                    reads=[bank, "s_tok"], writes=["rbuf"])
                fn(nb, eb)

            def q_stt(nb, eb):
                fw.op("dve", (lambda e, nb=nb, eb=eb: e.scalar_tensor_tensor(
                    S["qtok"][:, nb * 512:(nb + 1) * 512], S[eb][:], 1.0,
                    S["rbuf"][:], ALU.min, ALU.add)),
                    reads=[eb, "rbuf"], writes=["qtok"])

            qkv_mm(u2, sub, "wq", "Rwq", 0)
            elu1(0, q_stt)
            qkv_mm(u2, sub, "wq", "Rwq", 1)
            hk()
            elu1(1, q_stt)

            def k_stt(nb, eb):
                fw.op("dve", (lambda e, nb=nb, ke=ke, eb=eb: e.scalar_tensor_tensor(
                    S[ke][:, nb * 520:nb * 520 + 520].rearrange("p (h w) -> p h w", w=65)[:, :, 0:64],
                    S[eb][:].rearrange("p (h w) -> p h w", w=64), 1.0,
                    S["rbuf"][:].rearrange("p (h w) -> p h w", w=64), ALU.min, ALU.add)),
                    reads=[eb, "rbuf"], writes=[ke])

            qkv_mm(u2, sub, "wk", "Rwk", 0)
            elu1(0, k_stt)
            qkv_mm(u2, sub, "wk", "Rwk", 1)

            def qtr(e):
                last = None
                for jj in range(CH):
                    last = e.transpose(PS["b7"][:, jj * 128:(jj + 1) * 128],
                                       S["qtok"][:, jj * 128:(jj + 1) * 128],
                                       S["ident_bf"][:])
                return last
            fw.op("pe", qtr, reads=["qtok", "ident_bf"], writes=["b7"])
            hk()
            elu1(1, k_stt)
            fw.op("dve", (lambda e, g=g: e.tensor_copy(
                S["qfm"][:].rearrange("p (c t) -> p c t", c=CH)[:, :, g * 128:(g + 1) * 128],
                PS["b7"][:, :].rearrange("p (c t) -> p c t", c=CH))),
                reads=["b7"], writes=["qfm"])
            hk()

            def v_mul(nb):
                bank = f"b{1 + nb}"
                fw.op("act", (lambda e, bank=bank, g=g, ve=ve, nb=nb: e.mul(
                    S[ve][:, nb * 520:nb * 520 + 520].rearrange("p (h w) -> p h w", w=65)[:, :, 0:64],
                    PS[bank][:, 0:512].rearrange("p (h w) -> p h w", w=64),
                    S["s_tok"][:, g:g + 1])),
                    reads=[bank, "s_tok"], writes=[ve])

            qkv_mm(u2, sub, "wv", "Rwv", 0)
            v_mul(0)
            qkv_mm(u2, sub, "wv", "Rwv", 1)
            hk()
            v_mul(1)

            def kvmm(e, ke=ke, ve=ve, g=g):
                last = None
                for h in range(H):
                    bi = min(h // 7, 2)
                    hh = h - 7 * bi
                    last = e.matmul(PS[f"b{4 + bi}"][0:65, hh * 65:(hh + 1) * 65],
                                    S[ke][:, h * 65:h * 65 + 65],
                                    S[ve][:, h * 65:h * 65 + 65],
                                    start=(g % NT == 0 and hh == 0),
                                    stop=(g % NT == NT - 1))
                return last
            fw.op("pe", kvmm, reads=[ke, ve], writes=["b4", "b5", "b6"])
            hk()
        hk()

    apc = [attn_pieces(k) for k in range(NT)]
    for nm in ("P0", "P1", "P2", "P3", "P4", "C", "D1", "D2", "E", "AT1", "AT2"):
        apc[0][nm]()
    for k in range(NT):
        if k + 1 < NT:
            p = apc[k + 1]
            hooks = {
                0: [p["P0"]],
                1: [p["P1"]],
                2: [p["P2"]],
                3: [p["P3"]],
                4: [p["P4"]],
                5: [p["C"]],
                6: [p["D1"], p["D2"]],
                7: [p["E"]],
                8: [p["AT1"]],
                9: [p["AT2"]],
            }
        else:
            hooks = {}
        attn_B(k, hooks)
        if k == 0:
            fw.op("pool", lambda e: e.nop(),
                  reads=["rg", "kact", "kact1", "kact2", "kact3", "kact4",
                         "dbuf", "t2"], writes=["wp_ok"])
            wload("wp", "wpT", "Rwp", extra_reads=["wp_ok"])
        if k == 3:
            fw.op("dve", (lambda e: (
                e.tensor_copy(S["kv_f"][0:65, 0:455], PS["b4"][0:65, 0:455]),
                e.tensor_copy(S["kv_f"][0:65, 455:910], PS["b5"][0:65, 0:455]),
                e.tensor_copy(S["kv_f"][0:65, 910:1040], PS["b6"][0:65, 0:130]))[-1]),
                reads=["b4", "b5", "b6"], writes=["kv_f"])
            fw.dma("sp", lambda e: e.dma_start(out=cc_kv_in[0:65, :],
                                               in_=S["kv_f"][0:65, :]),
                   reads=["kv_f"], writes=["cc_kv_in"])
            fw.cc(lambda e: e.collective_compute(
                "AllReduce", ALU.add, replica_groups=PAIRS,
                ins=[cc_kv_in.ap().opt()], outs=[cc_kv_out.ap().opt()]),
                reads=["cc_kv_in"], writes=["cc_kv_out"])

    def kvevac(e):
        e.tensor_copy(S["kv_f"][0:65, 0:455], PS["b4"][0:65, 0:455])
        e.tensor_copy(S["kv_f"][0:65, 455:910], PS["b5"][0:65, 0:455])
        return e.tensor_copy(S["kv_f"][0:65, 910:1040], PS["b6"][0:65, 0:130])
    fw.op("dve", kvevac, reads=["b4", "b5", "b6"], writes=["kv_f"])
    if upto == "attn1":
        return finish_partial()
    fw.dma("sp", lambda e: e.dma_start(out=cc_kv_in2[0:65, :], in_=S["kv_f"][0:65, :]),
           reads=["kv_f"], writes=["cc_kv_in2"])
    fw.cc(lambda e: e.collective_compute(
        "AllReduce", ALU.add, replica_groups=PAIRS,
        ins=[cc_kv_in2.ap().opt()], outs=[cc_kv_out2.ap().opt()]),
        reads=["cc_kv_in2"], writes=["cc_kv_out2"])
    fw.dma("sp", lambda e: e.dma_start(out=S["kv_f"][0:65, :], in_=cc_kv_out[0:65, :]),
           reads=["cc_kv_out", "kv_f"], writes=["kv_f"])
    fw.dma("sp", lambda e: e.dma_start(out=S["kvtmp"][0:65, :], in_=cc_kv_out2[0:65, :]),
           reads=["cc_kv_out2"], writes=["kvtmp"])
    fw.op("dve", (lambda e: e.tensor_tensor(
        S["kv_f"][0:65, :], S["kv_f"][0:65, :], S["kvtmp"][0:65, :], ALU.add)),
        reads=["kvtmp"], writes=["kv_f"])
    fw.op("dve", lambda e: e.memset(S["kv_bf"][:], 0.0), writes=["kv_bf"])
    fw.dma("pool", lambda e: e.dma_start(
        out=S["kv_bf"][0:64, 0:1024].rearrange("p (j w) -> p j w", w=128)[:, :, 0:64],
        in_=S["kv_f"][0:64, :].rearrange("p (j w) -> p j w", w=130)[:, :, 0:64]),
        reads=["kv_f"], writes=["kv_bf"])
    fw.dma("pool", lambda e: e.dma_start(
        out=S["kv_bf"][64:128, 0:1024].rearrange("p (j w) -> p j w", w=128)[:, :, 64:128],
        in_=S["kv_f"][0:64, :].rearrange("p (j w) -> p j w", w=130)[:, :, 65:129]),
        reads=["kv_f"], writes=["kv_bf"])
    fw.dma("pool", lambda e: e.dma_start(
        out=S["kv_bf"][0:64, 1024:1040].rearrange("p (j w) -> p j w", w=2)[:, :, 0:1],
        in_=S["kv_f"][0:64, :].rearrange("p (j w) -> p j w", w=130)[:, :, 64:65],
        allow_slow_non_contiguous=True),
        reads=["kv_f"], writes=["kv_bf"])
    fw.dma("pool", lambda e: e.dma_start(
        out=S["kv_bf"][64:128, 1024:1040].rearrange("p (j w) -> p j w", w=2)[:, :, 1:2],
        in_=S["kv_f"][0:64, :].rearrange("p (j w) -> p j w", w=130)[:, :, 129:130],
        allow_slow_non_contiguous=True),
        reads=["kv_f"], writes=["kv_bf"])


    fw.op("pool", lambda e: e.nop(), writes=["Rwq", "Rwk", "Rwv"])
    fw.op("pool", lambda e: e.nop(),
          reads=["Rwq", "Rwk", "Rwv"], writes=["reload_ok"])
    RELOADS = [("wr2", "wrT", "Rwr2"), ("wkf2", "wkfT", "Rwkf2"),
               ("wvf2", "wvfT", "Rwvf2")]

    if upto == "kv":
        return finish_partial()

    SUBBANKS = [("b4", "b5", 0), ("b3", "b0", 16)]

    def a2_pmm(k, sub):
        g = 2 * k + sub
        bk0, bk1, d0 = SUBBANKS[sub]

        def f(e):
            last = None
            for j in range(H // 2):
                bank = bk0 if j < 4 else bk1
                c = (j % 4) * 128
                lhsT = S["qfm"][:, j * TCORE + g * 128:j * TCORE + (g + 1) * 128]
                e.matmul(PS[bank][:, c:c + 128], lhsT,
                         S["kv_bf"][:, j * 128:(j + 1) * 128],
                         start=True, stop=True)
                last = e.matmul(PS["b6"][:, d0 + 2 * j:d0 + 2 * j + 2], lhsT,
                                S["kv_bf"][:, 1024 + 2 * j:1024 + 2 * j + 2],
                                start=True, stop=True)
            return last
        fw.op("pe", f, reads=["qfm", "kv_bf"], writes=[bk0, bk1, f"b6d{sub}"])

    def a2_chain(k, sub):
        bk0, bk1, d0 = SUBBANKS[sub]
        zd = "zden" if sub == 0 else "zden1"
        zr = "zrec" if sub == 0 else "zrec1"
        pt = "ptok" if sub == 0 else "qtok"

        fw.op("dve", (lambda e: e.tensor_copy(S[zd][:], PS["b6"][:, d0:d0 + 16])),
              reads=[f"b6d{sub}"], writes=[zd])
        fw.op("act", (lambda e: e.activation(S[zd][:], S[zd][:],
                                             AF.Relu, bias=S["cepsa"][:, 0:1])),
              reads=["cepsa"], writes=[zd])
        fw.op("dve", (lambda e: e.reciprocal(S[zr][:], S[zd][:])),
              reads=[zd], writes=[zr], fence=True)

        def sc(bk, z0, pcol):
            fw.op("dve", (lambda e: e.tensor_tensor(
                S[pt][:, pcol:pcol + 512].rearrange("p (h w) -> p h w", w=64),
                PS[bk][:, 0:512].rearrange("p (h w) -> p h w", w=64),
                S[zr][:, z0:z0 + 8].unsqueeze(2).broadcast_to([128, 8, 64]),
                ALU.mult)),
                reads=[bk, zr], writes=[pt])
        sc(bk0, 0, 0)
        sc(bk1, 8, 512)

    def a2_ptr(k, sub):
        pt = "ptok" if sub == 0 else "qtok"

        def ptr(e):
            last = None
            for jj in range(CH):
                last = e.transpose(PS["b7"][:, jj * 128:(jj + 1) * 128],
                                   S[pt][:, jj * 128:(jj + 1) * 128],
                                   S["ident_bf"][:])
            return last
        fw.op("pe", ptr, reads=[pt, "ident_bf"], writes=["b7"])
        fw.op("dve", (lambda e, sub=sub: e.tensor_copy(
            S["pfm"][:].rearrange("p (c t) -> p c t", c=CH)[:, :, sub * 128:(sub + 1) * 128],
            PS["b7"][:, :].rearrange("p (c t) -> p c t", c=CH))),
            reads=["b7"], writes=["pfm"])

    def a2_proj(k, half):
        def f(e):
            last = None
            for oc in range(4):
                och = half * 4 + oc
                bank = PS[f"b{1 + oc // 2}"]
                cols = slice((oc % 2) * TILE, (oc % 2) * TILE + TILE)
                for j in range(CH):
                    last = e.matmul(bank[:, cols],
                                    S["wp"][:, j * C + och * 128: j * C + (och + 1) * 128],
                                    S["pfm"][:, j * TILE:(j + 1) * TILE],
                                    start=(j == 0 and oc % 2 == 0), stop=(j == CH - 1))
            return last
        fw.op("pe", f, reads=["pfm", "Rwp"], writes=["b1", "b2"])

    def a2_evb(k, half):
        c0 = k * TILE

        def f(e):
            last = None
            for oc in range(4):
                och = half * 4 + oc
                bank = PS[f"b{1 + oc // 2}"]
                cols = slice((oc % 2) * TILE, (oc % 2) * TILE + TILE)
                last = e.scalar_tensor_tensor(
                    Xv[:, och, c0:c0 + TILE], Xv[:, och, c0:c0 + TILE],
                    S["bias"][:, och:och + 1], bank[:, cols],
                    ALU.add, ALU.add)
            return last
        fw.op("dve", f, reads=["b1", "b2", "bias", f"X{k}"], writes=[f"X{k}"])

    def a2_boundary():
        fw.op("act", (lambda e: e.mul(
            S["bndx"][:].unsqueeze(2), Xv[:, :, TCORE - 1:TCORE], 1.0)),
            reads=[f"X{NT - 1}"], writes=["bndx"], fence=True)
        fw.op("dve", lambda e: e.tensor_tensor(S["bndx"][:], S["bndx"][:],
                                               S["om_b"][:], ALU.mult),
              reads=["bndx", "om_b"], writes=["bndx"], fence=True)
        fw.dma("sp", lambda e: e.dma_start(out=cc_b_in[:, :], in_=S["bndx"][:]),
               reads=["bndx"], writes=["cc_b_in"])
        fw.cc(lambda e: e.collective_compute(
            "AllReduce", ALU.add, replica_groups=PAIRS,
            ins=[cc_b_in.ap().opt()], outs=[cc_b_out.ap().opt()]),
            reads=["cc_b_in"], writes=["cc_b_out"])
        fw.dma("sp", lambda e: e.dma_start(out=S["halo2w"][:, :], in_=cc_b_out[:, :]),
               reads=["cc_b_out"], writes=["halo2w"])
        fw.op("dve", lambda e: e.tensor_tensor(S["halo2"][:], S["halo2w"][:],
                                               S["mask_b"][:], ALU.mult),
              reads=["halo2w", "mask_b"], writes=["halo2"], fence=True)

    a2_order = [NT - 1] + list(range(NT - 1))
    a2_pmm(a2_order[0], 0)
    a2_pmm(a2_order[0], 1)
    a2_chain(a2_order[0], 0)
    a2_chain(a2_order[0], 1)
    for i, k in enumerate(a2_order):
        if a2lim and i >= a2lim:
            break
        a2_ptr(k, 0)
        a2_ptr(k, 1)
        kn = a2_order[i + 1] if i + 1 < NT else None
        if kn is not None:
            a2_pmm(kn, 0)
            a2_pmm(kn, 1)
            a2_chain(kn, 0)
        a2_proj(k, 0)
        a2_evb(k, 0)
        if kn is not None:
            a2_chain(kn, 1)
        a2_proj(k, 1)
        a2_evb(k, 1)
        if 1 <= i <= 6:
            dst, wsrc, res = RELOADS[(i - 1) // 2]
            half = (i - 1) % 2
            wload(dst, wsrc, res, extra_reads=["reload_ok"],
                  j0=half * 4, j1=half * 4 + 4, fin=(half == 1))
            if i == 6:
                fw.op("dve", (lambda e: e.tensor_scalar(
                    S["wvf2"][:], S["wvf2"][:], 0.5, None, ALU.mult)),
                    reads=["Rwvf2"], writes=["Rwvf2"])
        if k == NT - 1:
            a2_boundary()
        if k == 0:
            fw.op("act", (lambda e: e.mul(
                S["hsrcX"][:].unsqueeze(2), Xv[:, :, TILE - 1:TILE], 1.0)),
                reads=["X0"], writes=["hsrcX"], fence=True)

    if upto == "attn2":
        return finish_partial()

    ffn_phase(list(range(1, NT)) + [0], {1: "hsrcX", 0: "halo2"},
              ("wr2", "Rwr2", "wkf2", "Rwkf2", "wvf2", "Rwvf2"), emit_out=True)

    with nc.Block() as block:
        fw.emit(block, stack)
    stack.close()
    return nc


_NC_CACHE = None


def kernel(L0_attn_x, L0_attn_Wqkv, L0_attn_Wproj, L0_attn_bproj,
           L0_ffn_W_r, L0_ffn_W_k, L0_ffn_W_v, L0_ffn_mu_r, L0_ffn_mu_k):
    global _NC_CACHE
    x = np.asarray(L0_attn_x, dtype=np.float32)
    Wqkv = np.asarray(L0_attn_Wqkv, dtype=np.float32)
    Wproj = np.asarray(L0_attn_Wproj, dtype=np.float32)
    bproj = np.asarray(L0_attn_bproj, dtype=np.float32)
    W_r = np.asarray(L0_ffn_W_r, dtype=np.float32)
    W_k = np.asarray(L0_ffn_W_k, dtype=np.float32)
    W_v = np.asarray(L0_ffn_W_v, dtype=np.float32)
    mu_r = np.asarray(L0_ffn_mu_r, dtype=np.float32)
    mu_k = np.asarray(L0_ffn_mu_k, dtype=np.float32)

    B, N, _ = x.shape
    shared = {
        "wqT": np.ascontiguousarray(Wqkv[0:C].T),
        "wkT": np.ascontiguousarray(Wqkv[C:2 * C].T),
        "wvT": np.ascontiguousarray(Wqkv[2 * C:3 * C].T),
        "wpT": np.ascontiguousarray(Wproj.T),
        "wrT": np.ascontiguousarray(W_r.T),
        "wkfT": np.ascontiguousarray(W_k.T),
        "wvfT": np.ascontiguousarray(W_v.T),
        "mur8": np.ascontiguousarray(mu_r.reshape(CH, 128)),
        "muk8": np.ascontiguousarray(mu_k.reshape(CH, 128)),
        "bias8": np.ascontiguousarray(bproj.reshape(CH, 128)),
    }
    in_maps = []
    for core in range(8):
        b, h = core // 2, core % 2
        if h == 0:
            xs = np.concatenate([x[b, 0:1], x[b, 0:TCORE]], axis=0)
        else:
            xs = x[b, TCORE - 1:N]
        m = dict(shared)
        m["xT"] = np.ascontiguousarray(xs.T)
        m["mask8"] = np.full((128, CH), float(h), dtype=np.float32)
        m["om8"] = np.full((128, CH), float(1 - h), dtype=np.float32)
        in_maps.append(m)

    if _NC_CACHE is None:
        _NC_CACHE = build_program()
    res = run_bass_kernel_spmd(_NC_CACHE, in_maps, core_ids=list(range(8)))
    out = np.empty((B, N, C), dtype=np.float32)
    for core in range(8):
        b, h = core // 2, core % 2
        out[b, h * TCORE:(h + 1) * TCORE, :] = res.results[core]["yT"].T
    return out
